# version 10
# speedup vs baseline: 1.4865x; 1.0289x over previous
import numpy as np
import ml_dtypes
from contextlib import ExitStack

import concourse.bass as bass
import concourse.mybir as mybir
from concourse.bass_utils import run_bass_kernel_spmd

F32 = mybir.dt.float32
BF16 = mybir.dt.bfloat16
I32 = mybir.dt.int32
I8 = mybir.dt.int8
AF = mybir.ActivationFunctionType
ALU = mybir.AluOpType
AX = mybir.AxisListType
NBF16 = ml_dtypes.bfloat16

D = 1024
DC = D // 128
EPS = 1e-5

ENG_NAMES = ("tensor", "vector", "scalar", "pool", "sync")


class SubT:

    __slots__ = ("t", "off", "p", "n")

    def __init__(self, t, off, p, n):
        self.t = t
        self.off = off
        self.p = p
        self.n = n

    def __getitem__(self, idx):
        if not isinstance(idx, tuple):
            idx = (idx,)
        ps = idx[0] if len(idx) > 0 else slice(None)
        fs = idx[1] if len(idx) > 1 else slice(None)
        p0 = ps.start if ps.start is not None else 0
        p1 = ps.stop if ps.stop is not None else self.p
        f0 = fs.start if fs.start is not None else 0
        f1 = fs.stop if fs.stop is not None else self.n
        return self.t[p0:p1, self.off + f0:self.off + f1]


class Op:
    __slots__ = ("fn", "inc", "deps", "waits", "dma_group", "tag", "label")

    def __init__(self, fn, dma_group=None, tag=""):
        self.label = ""
        self.fn = fn
        self.inc = False
        self.deps = {}
        self.waits = None
        self.dma_group = dma_group
        self.tag = tag


class Prog:

    def __init__(self, nc):
        self.nc = nc
        self.ops = {e: [] for e in ENG_NAMES}
        self.gctx = ExitStack()
        self.ctx = self.gctx
        self.esem = {e: self.gctx.enter_context(nc.semaphore(f"es_{e}"))
                     for e in ("tensor", "vector", "scalar", "pool")}
        self.dsem = {}
        self.dcnt = {}
        self.last_write = {}
        self.readers = {}
        self.sb_bytes = 0
        self.peak_log = []
        self.cur_label = "prologue"
        self._mk_arenas()

    ARENA_COLS = {F32: 10496, BF16: 80896, I8: 6144}

    def _mk_arenas(self):
        self.arena = {dt: self.gctx.enter_context(
            self.nc.sbuf_tensor(f"arena_{i}", [128, n], dt))
            for i, (dt, n) in enumerate(self.ARENA_COLS.items())}
        self.bump = {dt: 0 for dt in self.ARENA_COLS}
        self.psfw = self.gctx.enter_context(self.nc.psum_tensor("psfw", [128, 3072], F32))
        self.psb = [self.gctx.enter_context(self.nc.psum_tensor(f"psb{i}", [128, D], BF16))
                    for i in range(2)]
        self.ps_bump = {F32: 0, BF16: 0}

    def sb(self, name, shape, dt):
        assert len(shape) == 2
        cols = int(shape[1])
        off = self.bump[dt]
        assert off + cols <= self.ARENA_COLS[dt], (
            f"arena {dt} overflow: {name} needs {cols} at {off}")
        self.bump[dt] = off + cols
        self.sb_bytes += cols * {F32: 4, BF16: 2, I32: 4, I8: 1}[dt]
        return SubT(self.arena[dt], off, shape[0], cols)

    def ps(self, name, shape, dt):
        if dt == BF16:
            i = self.ps_bump[BF16]
            assert i < 2, "out of bf16 psum banks"
            self.ps_bump[BF16] = i + 1
            return SubT(self.psb[i], 0, shape[0], shape[1])
        nb = (shape[1] * 4 + 2047) // 2048
        i = self.ps_bump[F32]
        if nb > 1 and i % nb:
            i += nb - (i % nb)
        assert i + nb <= 6, f"out of fp32 psum banks ({name})"
        self.ps_bump[F32] = i + nb
        return SubT(self.psfw, i * 512, shape[0], shape[1])

    def dma_sem(self, group):
        if group not in self.dsem:
            self.dsem[group] = self.gctx.enter_context(self.nc.semaphore(f"dg_{group}"))
            self.dcnt[group] = 0
        return self.dsem[group]

    def scope(self, label=""):
        prog = self

        class _Scope:
            def __enter__(self):
                self.bytes0 = prog.sb_bytes
                self.bumps = dict(prog.bump)
                self.psb_ = dict(prog.ps_bump)
                self.lbl0 = prog.cur_label
                prog.cur_label = label
                return self

            def __exit__(self, *a):
                prog.peak_log.append((label, prog.sb_bytes - self.bytes0,
                                      dict(prog.bump)))
                prog.sb_bytes = self.bytes0
                prog.bump = self.bumps
                prog.ps_bump = self.psb_
                prog.cur_label = self.lbl0
                prog.barrier()
        return _Scope()

    def barrier(self):
        deps = {}
        for e in ("tensor", "vector", "scalar", "pool"):
            for i in range(len(self.ops[e]) - 1, -1, -1):
                if self.ops[e][i].fn is not None and self.ops[e][i].dma_group is None:
                    deps[("eng", e)] = i
                    break
        for g, c in self.dcnt.items():
            deps[("dma", g)] = c
        for e in ENG_NAMES:
            op = Op(None)
            op.deps = {k: v for k, v in deps.items() if k != ("eng", e)}
            self.ops[e].append(op)
        self.last_write = {}
        self.readers = {}

    def emit(self, eng, fn, reads=(), writes=(), dma_group=None, tag=""):
        op = Op(fn, dma_group=dma_group, tag=tag)
        op.label = self.cur_label
        idx = len(self.ops[eng])

        def add_dep(d):
            if d is None:
                return
            if d[0] == "dma":
                key = ("dma", d[1])
                op.deps[key] = max(op.deps.get(key, 0), self.dcnt[d[1]])
            else:
                e2, i2 = d
                if e2 == eng and (e2 == "tensor" or i2 == idx):
                    return
                key = ("eng", e2)
                op.deps[key] = max(op.deps.get(key, -1), i2)

        for t in reads:
            w = self.last_write.get(t)
            if w is not None:
                add_dep(w)
        for t in writes:
            w = self.last_write.get(t)
            if w is not None:
                add_dep(w)
            for r in self.readers.get(t, ()):
                add_dep(r)

        self.ops[eng].append(op)

        if dma_group is not None:
            self.dma_sem(group=dma_group)
            self.dcnt[dma_group] += 16
            token = ("dma", dma_group, self.dcnt[dma_group])
        else:
            token = (eng, idx)

        for t in writes:
            self.last_write[t] = token
            self.readers[t] = []
        for t in reads:
            self.readers.setdefault(t, []).append(token)
        return token

    def dma(self, out, in_, group, reads=(), writes=(), eng="sync", tag=""):
        def fn(e, out=out, in_=in_):
            return e.dma_start(out=out, in_=in_)
        return self.emit(eng, fn, reads=reads, writes=writes, dma_group=group, tag=tag)

    def dma_fence(self, group):
        tot = self.dcnt[group]
        for t, w in list(self.last_write.items()):
            if w[0] == "dma" and w[1] == group:
                self.last_write[t] = ("dma", group, tot)

    def finalize(self):
        for e in ENG_NAMES:
            for op in self.ops[e]:
                for key, val in op.deps.items():
                    if key[0] == "eng":
                        self.ops[key[1]][val].inc = True
        self.inc_prefix = {}
        for e in ("tensor", "vector", "scalar", "pool"):
            pc, c = [], 0
            for op in self.ops[e]:
                if op.dma_group is None and op.inc:
                    c += 1
                pc.append(c)
            self.inc_prefix[e] = pc
        for e in ENG_NAMES:
            waited = {}
            for op in self.ops[e]:
                ws = []
                for key, val in op.deps.items():
                    if key[0] == "dma":
                        sem, v = self.dsem[key[1]], val
                        wkey = ("d", key[1])
                    else:
                        sem = self.esem[key[1]]
                        v = self.inc_prefix[key[1]][val]
                        wkey = ("e", key[1])
                    if waited.get(wkey, -1) >= v:
                        continue
                    waited[wkey] = v
                    ws.append((sem, v))
                op.waits = ws

    def run_engine(self, e, engine):
        sem = self.esem.get(e)
        cur = None
        ctx = None
        for op in self.ops[e]:
            if op.label != cur:
                if ctx is not None:
                    ctx.__exit__(None, None, None)
                ctx = self.nc.named_scope(op.label or "x")
                ctx.__enter__()
                cur = op.label
            for s, v in op.waits:
                engine.wait_ge(s, v)
            if op.fn is None:
                continue
            inst = op.fn(engine)
            if op.dma_group is not None:
                inst.then_inc(self.dsem[op.dma_group], 16)
            elif op.inc:
                inst.then_inc(sem, 1)
        if ctx is not None:
            ctx.__exit__(None, None, None)

    def stats(self):
        return {e: len(self.ops[e]) for e in ENG_NAMES} | {
            "sb_bytes_per_part": self.sb_bytes}

    def run(self):
        self.finalize()
        nc = self.nc
        with nc.Block() as block:
            @block.sync
            def _(sync):
                self.run_engine("sync", sync)
                for g, tot in sorted(self.dcnt.items()):
                    sync.wait_ge(self.dsem[g], tot)

            @block.tensor
            def _(tensor):
                self.run_engine("tensor", tensor)

            @block.vector
            def _(vector):
                self.run_engine("vector", vector)

            @block.scalar
            def _(scalar):
                self.run_engine("scalar", scalar)

            @block.gpsimd
            def _(gpsimd):
                self.run_engine("pool", gpsimd)



class Common:
    def __init__(self, p, consts_dram):
        self.p = p
        self.ident = p.sb("c_ident", [128, 128], BF16)
        self.ones_row = p.sb("c_ones_row", [1, 128], BF16)
        self.smalls = p.sb("c_smalls", [128, 8], F32)
        self.masks = p.sb("c_masks", [128, 3 * 128], BF16)
        self.triu_if = p.sb("c_triuif", [128, 128], F32)
        self.idf32 = p.sb("c_idf32", [128, 128], F32)
        self.ones_rowf = p.sb("c_ones_rowf", [1, 128], F32)
        cd = consts_dram
        p.dma(self.ident[:], cd["c_ident"][:], "consts", writes=["c_ident"])
        p.dma(self.ones_row[:], cd["c_ones_row"][:], "consts", writes=["c_ones_row"])
        p.dma(self.smalls[:], cd["c_smalls"][:], "consts", writes=["c_smalls"])
        p.dma(self.masks[:], cd["c_masks"][:], "consts", writes=["c_masks"])
        p.dma(self.triu_if[:], cd["c_triuif"][:], "consts", writes=["c_triuif"])
        p.dma(self.idf32[:], cd["c_idf32"][:], "consts", writes=["c_idf32"])
        p.dma(self.ones_rowf[:], cd["c_ones_rowf"][:], "consts", writes=["c_ones_rowf"])
        p.dma_fence("consts")
        self.eps_col = self.smalls[:, 0:1]
        self.eps6_col = self.smalls[:, 1:2]
        self.tril_s = self.masks[:, 0:128]
        self.triu_s = self.masks[:, 128:256]
        self.triu_i = self.masks[:, 256:384]
        self.sq2 = [p.sb(f"ln_sq{i}", [128, D], BF16) for i in range(2)]
        self.st2 = [p.sb(f"ln_st{i}", [128, 8], F32) for i in range(2)]
        self.sq = self.sq2[0]
        self.ln_i = 0

    def ln_tm(self, xin_ap, xtag, yout_ap, ytag):
        p = self.p
        li = self.ln_i = 1 - self.ln_i
        s = self.st2[li]
        sq = self.sq2[li]
        t = lambda n: f"ln{li}_{n}"
        p.emit("vector", lambda e, a=xin_ap, s=s: e.reduce_sum(out=s[:, 0:1], in_=a, axis=AX.X),
               reads=[xtag], writes=[t("s0")])
        p.emit("scalar", lambda e, a=xin_ap, s=s, sq=sq: e.activation(
            out=sq[:], in_=a, func=AF.Square, accum_out=s[:, 1:2]),
            reads=[xtag], writes=[t("sq"), t("s1x")])
        p.emit("scalar", lambda e, s=s: e.activation(out=s[:, 7:8], in_=s[:, 1:2],
                                                     func=AF.Identity),
               reads=[t("s1x")], writes=[t("s1")])
        p.emit("vector", lambda e, s=s: e.tensor_scalar(out=s[:, 2:3], in0=s[:, 0:1],
                                                        scalar1=1.0 / D, scalar2=None, op0=ALU.mult),
               reads=[t("s0")], writes=[t("m")])
        p.emit("vector", lambda e, s=s: e.tensor_scalar(out=s[:, 3:4], in0=s[:, 7:8],
                                                        scalar1=1.0 / D, scalar2=None, op0=ALU.mult),
               reads=[t("s1")], writes=[t("ms")])
        p.emit("vector", lambda e, s=s: e.scalar_tensor_tensor(
            out=s[:, 4:5], in0=s[:, 2:3], scalar=s[:, 2:3], in1=s[:, 3:4],
            op0=ALU.mult, op1=ALU.subtract),
            reads=[t("m"), t("ms")], writes=[t("nv")])
        p.emit("scalar", lambda e, s=s: e.activation(out=s[:, 5:6], in_=s[:, 4:5], func=AF.Ln,
                                                     bias=self.eps_col, scale=-1.0),
               reads=[t("nv"), "c_smalls"], writes=[t("t1")])
        p.emit("scalar", lambda e, s=s: e.activation(out=s[:, 6:7], in_=s[:, 5:6], func=AF.Exp,
                                                     scale=-0.5),
               reads=[t("t1")], writes=[t("r")])
        p.emit("vector", lambda e, a=xin_ap, b=yout_ap, s=s: e.tensor_scalar(
            out=b, in0=a, scalar1=s[:, 2:3], scalar2=s[:, 6:7],
            op0=ALU.subtract, op1=ALU.mult),
            reads=[xtag, t("m"), t("r")], writes=[ytag])

    def transpose_tile(self, src_ap, src_tag, ps_ap, ps_tag, dst_ap, dst_tag, nch=DC):
        p = self.p
        for c in range(nch):
            p.emit("tensor", lambda e, c=c, s=src_ap, d=ps_ap: e.transpose(
                out=d[:, c * 128:(c + 1) * 128], in_=s[:, c * 128:(c + 1) * 128],
                identity=self.ident[:]),
                reads=[src_tag, "c_ident"], writes=[ps_tag])
        p.emit("vector", lambda e, s=ps_ap, d=dst_ap, nch=nch: e.tensor_copy(
            out=d, in_=s.rearrange("p (c f) -> p c f", c=nch)),
            reads=[ps_tag], writes=[dst_tag])



def emit_l0_ffn(p, com, W, xin, xout, T):
    with p.scope("l0_ffn"):
        H3 = 3072
        NJ = H3 // 128
        QT = min(T, 1024)
        NQ = T // QT
        nq128 = QT // 128
        n512 = QT // 512

        xt = [p.sb(f"f_xt{i}", [128, D], F32) for i in range(2)]
        y_tm = p.sb("f_ytm", [128, D], BF16)
        y_fm = p.sb("f_yfm", [128, DC * QT], BF16)
        hid = p.sb("f_hid", [128, NJ * QT], BF16)
        wgu8 = [p.sb(f"f_wgu8_{i}", [128, 2 * D], I8) for i in range(2)]
        wg_bf = [p.sb(f"f_wgbf{i}", [128, D], BF16) for i in range(2)]
        wu_bf = [p.sb(f"f_wubf{i}", [128, D], BF16) for i in range(2)]
        wd8s = [p.sb(f"f_wd8s{i}", [128, D], I8) for i in range(2)]
        wd_bf = p.sb("f_wdbf", [128, NJ * D], BF16)
        sg = p.sb("f_sg", [128, 512], BF16)
        su = p.sb("f_su", [128, 512], BF16)
        ot = [p.sb(f"f_ot{i}", [128, D], F32) for i in range(2)]
        lnout = p.sb("f_lnout", [128, D], F32)

        ps_t = p.ps("f_pst", [128, D], BF16)
        pg = [p.ps(f"f_pg{i}", [128, 512], F32) for i in range(2)]
        pu = [p.ps(f"f_pu{i}", [128, 512], F32) for i in range(2)]
        po = [p.ps(f"f_po{i}", [128, 512], F32) for i in range(2)]

        for j in range(NJ):
            b8 = wd8s[j % 2]
            p.dma(b8[:], W["l0f_wd"][j * 128:(j + 1) * 128, :], f"fwd{j%2}",
                  writes=[f"wd8s{j%2}"])
            p.emit("vector", lambda e, j=j, b8=b8: e.tensor_scalar(
                out=wd_bf[:, j * D:(j + 1) * D], in0=b8[:],
                scalar1=1.0 / 127.0, scalar2=None, op0=ALU.mult),
                reads=[f"wd8s{j%2}"], writes=[f"wdbf_{j}"])

        yfm3 = y_fm[:].rearrange("p (c t) -> p c t", c=DC)

        for q in range(NQ):
            t0q = q * QT
            for i in range(nq128):
                ti = t0q // 128 + i
                b = xt[i % 2]
                p.dma(b[:], xin[ti * 128:(ti + 1) * 128, :], f"fxin{i%2}", writes=[f"fxt{i%2}"])
                com.ln_tm(b[:], f"fxt{i%2}", y_tm[:], "fytm")
                com.transpose_tile(y_tm[:], "fytm", ps_t[:], "fpst",
                                   yfm3[:, :, i * 128:(i + 1) * 128], "yfm_q")
            for j in range(NJ):
                wb = wgu8[j % 2]
                gb, ub = wg_bf[j % 2], wu_bf[j % 2]
                wb3a = wb[:, 0:D].rearrange("p (c f) -> p c f", f=128)
                wb3b = wb[:, D:2 * D].rearrange("p (c f) -> p c f", f=128)
                p.dma(wb3a, W["l0f_wg"][:, j * 128:(j + 1) * 128]
                      .rearrange("(c p) f -> p c f", p=128), f"fw{j%2}",
                      writes=[f"fwgu8g_{j%2}"])
                p.dma(wb3b, W["l0f_wu"][:, j * 128:(j + 1) * 128]
                      .rearrange("(c p) f -> p c f", p=128), f"fw{j%2}",
                      writes=[f"fwgu8u_{j%2}"])
                p.emit("vector", lambda e, wb=wb, gb=gb: e.tensor_scalar(
                    out=gb[:], in0=wb[:, 0:D], scalar1=1.0 / 127.0, scalar2=None, op0=ALU.mult),
                    reads=[f"fwgu8g_{j%2}"], writes=[f"fwgbf{j%2}"])
                p.emit("vector", lambda e, wb=wb, ub=ub: e.tensor_scalar(
                    out=ub[:], in0=wb[:, D:2 * D], scalar1=1.0 / 127.0, scalar2=None, op0=ALU.mult),
                    reads=[f"fwgu8u_{j%2}"], writes=[f"fwubf{j%2}"])
                for s5 in range(n512):
                    pgb, pub = pg[s5 % 2], pu[s5 % 2]
                    for c in range(DC):
                        p.emit("tensor", lambda e, c=c, gb=gb, pgb=pgb, s5=s5: e.matmul(
                            out=pgb[:], lhsT=gb[:, c * 128:(c + 1) * 128],
                            rhs=y_fm[:, c * QT + s5 * 512: c * QT + s5 * 512 + 512],
                            start=(c == 0), stop=(c == DC - 1)),
                            reads=[f"fwgbf{j%2}", "yfm_q"], writes=[f"fpg{s5%2}"])
                    for c in range(DC):
                        p.emit("tensor", lambda e, c=c, ub=ub, pub=pub, s5=s5: e.matmul(
                            out=pub[:], lhsT=ub[:, c * 128:(c + 1) * 128],
                            rhs=y_fm[:, c * QT + s5 * 512: c * QT + s5 * 512 + 512],
                            start=(c == 0), stop=(c == DC - 1)),
                            reads=[f"fwubf{j%2}", "yfm_q"], writes=[f"fpu{s5%2}"])
                    p.emit("scalar", lambda e, pgb=pgb: e.activation(
                        out=sg[:], in_=pgb[:], func=AF.Silu),
                        reads=[f"fpg{s5%2}"], writes=["fsg"])
                    p.emit("vector", lambda e, pub=pub: e.tensor_copy(out=su[:], in_=pub[:]),
                           reads=[f"fpu{s5%2}"], writes=["fsu"])
                    p.emit("vector", lambda e, j=j, s5=s5: e.tensor_tensor(
                        out=hid[:, j * QT + s5 * 512: j * QT + s5 * 512 + 512],
                        in0=sg[:], in1=su[:], op=ALU.mult),
                        reads=["fsg", "fsu"], writes=[f"fhid{j}"])
            for i in range(nq128):
                ti = t0q // 128 + i
                ob = ot[i % 2]
                for dh in range(2):
                    pob = po[dh]
                    for j in range(NJ):
                        p.emit("tensor", lambda e, j=j, i=i, dh=dh, pob=pob: e.matmul(
                            out=pob[:], lhsT=hid[:, j * QT + i * 128: j * QT + i * 128 + 128],
                            rhs=wd_bf[:, j * D + dh * 512: j * D + dh * 512 + 512],
                            start=(j == 0), stop=(j == NJ - 1)),
                            reads=[f"fhid{j}", f"wdbf_{j}"], writes=[f"fpo{dh}"])
                    p.emit("vector", lambda e, dh=dh, ob=ob, pob=pob: e.tensor_copy(
                        out=ob[:, dh * 512:(dh + 1) * 512], in_=pob[:]),
                        reads=[f"fpo{dh}"], writes=[f"fot{i%2}"])
                com.ln_tm(ob[:], f"fot{i%2}", lnout[:], "flnout")
                b = xt[i % 2]
                p.dma(b[:], xin[ti * 128:(ti + 1) * 128, :], f"fxin{i%2}", writes=[f"fxt{i%2}"])
                p.emit("vector", lambda e, b=b: e.tensor_tensor(
                    out=b[:], in0=b[:], in1=lnout[:], op=ALU.add),
                    reads=[f"fxt{i%2}", "flnout"], writes=[f"fxt{i%2}"])
                p.dma(xout[ti * 128:(ti + 1) * 128, :], b[:], "fxout", reads=[f"fxt{i%2}"])





def emit_l0_attn(p, com, W, xin, xout, z1d, T):
    NT = T // 128
    H = 8

    with p.scope("l0_attn_rec"):
        wq = p.sb("a_wq", [128, DC * D], BF16)
        wk = p.sb("a_wk", [128, DC * D], BF16)
        wv = p.sb("a_wv", [128, DC * D], BF16)
        wo = p.sb("a_wo", [128, DC * D], BF16)
        wab = p.sb("a_wab", [128, DC * 16], BF16)
        ln1 = p.sb("a_ln1", [128, 16], F32)
        wld = p.sb("a_wld", [128, D], F32)
        for name, dst in (("l0a_wq", wq), ("l0a_wk", wk), ("l0a_wv", wv), ("l0a_wo", wo)):
            for c in range(DC):
                p.dma(wld[:], W[name][c * 128:(c + 1) * 128, :], "awl",
                      writes=["a_wld"])
                p.emit("vector", lambda e, dst=dst, c=c: e.tensor_copy(
                    out=dst[:, c * D:(c + 1) * D], in_=wld[:]),
                    reads=["a_wld"], writes=[f"{name}_bf"])
        p.dma(wld[:, 0:16 * DC].rearrange("p (c f) -> p c f", f=16),
              W["l0a_wab"][:, :].rearrange("(c p) f -> p c f", p=128), "awl",
              writes=["a_wld"])
        p.dma(ln1[:], W["l0a_ln1"][:], "amisc", writes=["a_ln1"])
        p.dma_fence("amisc")
        p.emit("vector", lambda e: e.tensor_copy(out=wab[:], in_=wld[:, 0:16 * DC]),
               reads=["a_wld"], writes=["a_wab"])

        xt = [p.sb(f"a_xt{i}", [128, D], F32) for i in range(2)]
        zt = [p.sb(f"a_zt{i}", [128, D], F32) for i in range(2)]
        n_tm = p.sb("a_ntm", [128, D], BF16)
        n_fm = p.sb("a_nfm", [128, D], BF16)
        qs_tm = p.sb("a_qstm", [128, D], BF16)
        ks_tm = p.sb("a_kstm", [128, D], BF16)
        v_tm = [p.sb(f"a_vtm{i}", [128, D], BF16) for i in range(2)]
        qn_tm = [p.sb(f"a_qntm{i}", [128, D], BF16) for i in range(2)]
        kn_tm = [p.sb(f"a_kntm{i}", [128, D], BF16) for i in range(2)]
        kb_tm = [p.sb(f"a_kbtm{i}", [128, D], BF16) for i in range(2)]
        kp_tm = [p.sb(f"a_kptm{i}", [128, D], BF16) for i in range(2)]
        q_fm = [p.sb(f"a_qfm{i}", [128, D], BF16) for i in range(2)]
        k_fm = [p.sb(f"a_kfm{i}", [128, D], BF16) for i in range(2)]
        kb_fm = [p.sb(f"a_kbfm{i}", [128, D], BF16) for i in range(2)]
        o_tm = [p.sb(f"a_otm{i}", [128, D], BF16) for i in range(2)]
        o_fm = p.sb("a_ofm", [128, D], BF16)
        ab_tm = p.sb("a_abtm", [128, 16], F32)
        la_tm = p.sb("a_latm", [128, 8], F32)
        L_tm = [p.sb(f"a_Ltm{i}", [128, 8], F32) for i in range(2)]
        gam = [p.sb(f"a_gam{i}", [128, 8], F32) for i in range(2)]
        ngam = [p.sb(f"a_ngam{i}", [128, 8], F32) for i in range(2)]
        wp_tm = [p.sb(f"a_wptm{i}", [128, 8], F32) for i in range(2)]
        gC = [p.sb(f"a_gC{i}", [128, 8], F32) for i in range(2)]
        LCc = p.sb("a_LCc", [8, 1], F32)
        LCr = p.sb("a_LCr", [1, 8], F32)
        LCb = [p.sb(f"a_LCb{i}", [128, 8], F32) for i in range(2)]
        LT8 = [p.sb(f"a_LT{h}", [1, 128], F32) for h in range(8)]
        nLT8 = [p.sb(f"a_nLT{h}", [1, 128], F32) for h in range(8)]
        qkss = p.sb("a_qkss", [128, 16], F32)
        expTs = p.sb("a_expTs", [128, H * 128], BF16)
        expLs = p.sb("a_expLs", [128, H * 128], BF16)
        GTs = p.sb("a_GTs", [128, H * 128], BF16)
        GTi = p.sb("a_GTi", [128, H * 128], BF16)
        GLs = p.sb("a_GLs", [128, H * 128], BF16)
        M_T = [p.sb(f"a_MT{h}", [128, 128], BF16) for h in range(H)]
        M_ = [p.sb(f"a_M{h}", [128, 128], BF16) for h in range(H)]
        A_T = [p.sb(f"a_AT{h}", [128, 128], BF16) for h in range(H)]
        Xb = [p.sb(f"a_X{h}", [128, 128], BF16) for h in range(H)]
        Pw = [p.sb(f"a_P{h}", [128, 128], BF16) for h in range(H)]
        PwT = [p.sb(f"a_PT{h}", [128, 128], BF16) for h in range(H)]
        Og = p.sb("a_Og", [128, 128], BF16)
        S_f = [p.sb(f"a_S{i}", [128, H * 128], F32) for i in range(2)]
        S_bf = p.sb("a_Sbf", [128, H * 128], BF16)

        pb = [p.ps(f"a_pb{i}", [128, 512], F32) for i in range(6)]
        pst = [p.ps(f"a_pst{i}", [128, D], BF16) for i in range(2)]

        p.emit("vector", lambda e: e.memset(S_f[0][:], 0.0), writes=["a_Sf0"])
        p.emit("vector", lambda e: e.memset(S_bf[:], 0.0), writes=["a_Sbf"])

        def tT(src_ap, stag, dst_ap, dtag, pi, affine=None):
            ps_ap = pst[pi]
            for c in range(DC):
                p.emit("tensor", lambda e, c=c, s=src_ap, d=ps_ap: e.transpose(
                    out=d[:, c * 128:(c + 1) * 128], in_=s[:, c * 128:(c + 1) * 128],
                    identity=com.ident[:]),
                    reads=[stag, "c_ident"], writes=[f"a_pst{pi}"])
            if affine is None:
                p.emit("vector", lambda e, d=dst_ap, ps_ap=ps_ap: e.tensor_copy(
                    out=d.rearrange("p (c f) -> p c f", c=DC),
                    in_=ps_ap[:].rearrange("p (c f) -> p c f", c=DC)),
                    reads=[f"a_pst{pi}"], writes=[dtag])
            else:
                wcol, bcol, wtag = affine
                for c in range(DC):
                    p.emit("vector", lambda e, c=c, d=dst_ap, ps_ap=ps_ap: e.tensor_scalar(
                        out=d[:, c * 128:(c + 1) * 128], in0=ps_ap[:, c * 128:(c + 1) * 128],
                        scalar1=wcol[:, c:c + 1], scalar2=bcol[:, c:c + 1],
                        op0=ALU.mult, op1=ALU.add),
                        reads=[f"a_pst{pi}", wtag], writes=[dtag])

        for ci in range(NT):
            par = ci % 2
            xb, zb = xt[par], zt[par]
            vtm, qtm, ktm, kbm, kpm = (v_tm[par], qn_tm[par], kn_tm[par],
                                       kb_tm[par], kp_tm[par])
            qfm, kfm, kbf, otm = q_fm[par], k_fm[par], kb_fm[par], o_tm[par]
            Ltm, gm, ngm, wpm, gCm, LCbm = (L_tm[par], gam[par], ngam[par],
                                            wp_tm[par], gC[par], LCb[par])
            p.dma(xb[:], xin[ci * 128:(ci + 1) * 128, :], f"ax{par}", writes=[f"a_xt{par}"])
            com.ln_tm(xb[:], f"a_xt{par}", zb[:], f"a_zt{par}")
            com.ln_tm(zb[:], f"a_zt{par}", n_tm[:], "a_ntm")
            tT(n_tm[:], "a_ntm", n_fm[:], "a_nfm", par,
               affine=(ln1[:, 0:8], ln1[:, 8:16], "a_ln1"))
            for (wsb, wtag, dst, dtag, act) in (
                    (wq, "l0a_wq_bf", qs_tm, "a_qstm", AF.Silu),
                    (wk, "l0a_wk_bf", ks_tm, "a_kstm", AF.Silu),
                    (wv, "l0a_wv_bf", vtm, f"a_vtm{par}", AF.Silu)):
                for kh in range(2):
                    pbk = pb[kh]
                    for c in range(DC):
                        p.emit("tensor", lambda e, c=c, kh=kh, wsb=wsb, pbk=pbk: e.matmul(
                            out=pbk[:], lhsT=n_fm[:, c * 128:(c + 1) * 128],
                            rhs=wsb[:, c * D + kh * 512: c * D + kh * 512 + 512],
                            start=(c == 0), stop=(c == DC - 1)),
                            reads=["a_nfm", wtag], writes=[f"a_pb{kh}"])
                    p.emit("scalar", lambda e, kh=kh, dst=dst, act=act, pbk=pbk: e.activation(
                        out=dst[:, kh * 512:(kh + 1) * 512], in_=pbk[:], func=act),
                        reads=[f"a_pb{kh}"], writes=[dtag])
            for c in range(DC):
                p.emit("tensor", lambda e, c=c: e.matmul(
                    out=pb[2][:, 0:16], lhsT=n_fm[:, c * 128:(c + 1) * 128],
                    rhs=wab[:, c * 16:(c + 1) * 16],
                    start=(c == 0), stop=(c == DC - 1)),
                    reads=["a_nfm", "a_wab"], writes=["a_pb2"])
            p.emit("scalar", lambda e: e.activation(out=ab_tm[:], in_=pb[2][:, 0:16],
                                                    func=AF.Sigmoid),
                   reads=["a_pb2"], writes=["a_abtm"])
            for (srct, stag, col0) in ((qs_tm, "a_qstm", 0), (ks_tm, "a_kstm", 8)):
                for h in range(H):
                    p.emit("scalar", lambda e, h=h, srct=srct, col0=col0: e.activation(
                        out=com.sq2[col0 // 8][:, h * 128:(h + 1) * 128],
                        in_=srct[:, h * 128:(h + 1) * 128], func=AF.Square,
                        accum_out=qkss[:, col0 + h:col0 + h + 1]),
                        reads=[stag], writes=["ln_sq", f"a_qkss{col0}_{h}"])
            for col0 in (0, 8):
                p.emit("scalar", lambda e, col0=col0: e.activation(
                    out=qkss[:, col0:col0 + 8], in_=qkss[:, col0:col0 + 8], func=AF.Ln,
                    bias=com.eps6_col),
                    reads=[f"a_qkss{col0}_0"] + [f"a_qkss{col0}_{h}" for h in range(H)],
                    writes=[f"a_rss{col0}"])
                p.emit("scalar", lambda e, col0=col0: e.activation(
                    out=qkss[:, col0:col0 + 8], in_=qkss[:, col0:col0 + 8], func=AF.Exp,
                    scale=-0.5),
                    reads=[f"a_rss{col0}"], writes=[f"a_rss{col0}"])
            for h in range(H):
                p.emit("vector", lambda e, h=h, qtm=qtm: e.tensor_scalar(
                    out=qtm[:, h * 128:(h + 1) * 128], in0=qs_tm[:, h * 128:(h + 1) * 128],
                    scalar1=qkss[:, h:h + 1], scalar2=None, op0=ALU.mult),
                    reads=["a_qstm", "a_rss0"], writes=[f"a_qntm{par}"])
                p.emit("vector", lambda e, h=h, ktm=ktm: e.tensor_scalar(
                    out=ktm[:, h * 128:(h + 1) * 128], in0=ks_tm[:, h * 128:(h + 1) * 128],
                    scalar1=qkss[:, 8 + h:8 + h + 1], scalar2=None, op0=ALU.mult),
                    reads=["a_kstm", "a_rss8"], writes=[f"a_kntm{par}"])
                p.emit("vector", lambda e, h=h, ktm=ktm, kbm=kbm: e.tensor_scalar(
                    out=kbm[:, h * 128:(h + 1) * 128], in0=ktm[:, h * 128:(h + 1) * 128],
                    scalar1=ab_tm[:, h:h + 1], scalar2=None, op0=ALU.mult),
                    reads=[f"a_kntm{par}", "a_abtm"], writes=[f"a_kbtm{par}"])
            tT(qtm[:], f"a_qntm{par}", qfm[:], f"a_qfm{par}", par)
            tT(ktm[:], f"a_kntm{par}", kfm[:], f"a_kfm{par}", 1 - par)
            tT(kbm[:], f"a_kbtm{par}", kbf[:], f"a_kbfm{par}", par)
            p.emit("scalar", lambda e: e.activation(out=la_tm[:], in_=ab_tm[:, 8:16],
                                                    func=AF.Ln),
                   reads=["a_abtm"], writes=["a_latm"])
            p.emit("tensor", lambda e: e.matmul(out=pb[2][:, 0:8], lhsT=com.triu_if[:],
                                                rhs=la_tm[:], start=True, stop=True),
                   reads=["a_latm", "c_triuif"], writes=["a_pb2"])
            p.emit("vector", lambda e, Ltm=Ltm: e.tensor_copy(out=Ltm[:], in_=pb[2][:, 0:8]),
                   reads=["a_pb2"], writes=[f"a_Ltm{par}"])
            p.emit("scalar", lambda e, Ltm=Ltm, gm=gm: e.activation(
                out=gm[:], in_=Ltm[:], func=AF.Exp),
                reads=[f"a_Ltm{par}"], writes=[f"a_gam{par}"])
            p.emit("vector", lambda e, gm=gm, ngm=ngm: e.tensor_scalar(
                out=ngm[:], in0=gm[:], scalar1=-1.0, scalar2=None, op0=ALU.mult),
                reads=[f"a_gam{par}"], writes=[f"a_ngam{par}"])
            p.emit("tensor", lambda e: e.matmul(out=pb[2][0:8, 128:129], lhsT=la_tm[:],
                                                rhs=com.smalls[:, 2:3], start=True, stop=True),
                   reads=["a_latm", "c_smalls"], writes=["a_pb2"])
            p.emit("vector", lambda e: e.tensor_copy(out=LCc[:], in_=pb[2][0:8, 128:129]),
                   reads=["a_pb2"], writes=["a_LCc"])
            p.emit("tensor", lambda e: e.matmul(out=pb[2][0:1, 136:144], lhsT=LCc[:],
                                                rhs=com.idf32[0:8, 0:8], start=True, stop=True),
                   reads=["a_LCc", "c_idf32"], writes=["a_pb2"])
            p.emit("vector", lambda e: e.tensor_copy(out=LCr[:], in_=pb[2][0:1, 136:144]),
                   reads=["a_pb2"], writes=["a_LCr"])
            p.emit("tensor", lambda e: e.matmul(out=pb[2][:, 144:152], lhsT=com.ones_rowf[0:1, :],
                                                rhs=LCr[:], start=True, stop=True),
                   reads=["a_LCr", "c_ones_rowf"], writes=["a_pb2"])
            p.emit("vector", lambda e, LCbm=LCbm: e.tensor_copy(out=LCbm[:], in_=pb[2][:, 144:152]),
                   reads=["a_pb2"], writes=[f"a_LCb{par}"])
            p.emit("vector", lambda e, LCbm=LCbm, Ltm=Ltm, wpm=wpm: e.tensor_tensor(
                out=wpm[:], in0=LCbm[:], in1=Ltm[:], op=ALU.subtract),
                reads=[f"a_LCb{par}", f"a_Ltm{par}"], writes=[f"a_wptm{par}"])
            p.emit("scalar", lambda e, wpm=wpm: e.activation(out=wpm[:], in_=wpm[:], func=AF.Exp),
                   reads=[f"a_wptm{par}"], writes=[f"a_wptm{par}"])
            p.emit("vector", lambda e, wpm=wpm: e.tensor_tensor(
                out=wpm[:], in0=wpm[:], in1=ab_tm[:, 0:8], op=ALU.mult),
                reads=[f"a_wptm{par}", "a_abtm"], writes=[f"a_wptm{par}"])
            p.emit("scalar", lambda e, LCbm=LCbm, gCm=gCm: e.activation(
                out=gCm[:], in_=LCbm[:], func=AF.Exp),
                reads=[f"a_LCb{par}"], writes=[f"a_gC{par}"])
            for h in range(H):
                p.emit("vector", lambda e, h=h, ktm=ktm, kpm=kpm, wpm=wpm: e.tensor_scalar(
                    out=kpm[:, h * 128:(h + 1) * 128], in0=ktm[:, h * 128:(h + 1) * 128],
                    scalar1=wpm[:, h:h + 1], scalar2=None, op0=ALU.mult),
                    reads=[f"a_kntm{par}", f"a_wptm{par}"], writes=[f"a_kptm{par}"])
            for h in range(H):
                p.emit("tensor", lambda e, h=h, Ltm=Ltm: e.transpose(
                    out=pb[2][0:1, 256:384], in_=Ltm[:, h:h + 1],
                    identity=com.idf32[:]),
                    reads=[f"a_Ltm{par}", "c_idf32"], writes=["a_pb2"])
                p.emit("vector", lambda e, h=h: e.tensor_copy(
                    out=LT8[h][:], in_=pb[2][0:1, 256:384]),
                    reads=["a_pb2"], writes=[f"a_LT{h}"])
                p.emit("vector", lambda e, h=h: e.tensor_scalar(
                    out=nLT8[h][:], in0=LT8[h][:], scalar1=-1.0, scalar2=None, op0=ALU.mult),
                    reads=[f"a_LT{h}"], writes=[f"a_nLT{h}"])

            Ssrc, Sdst = S_f[par], S_f[1 - par]
            stag_src, stag_dst = f"a_Sf{par}", f"a_Sf{1-par}"
            for h in range(H):
                hp = h % 2
                sc = pb[hp]
                bc = pb[2 + hp]
                p.emit("tensor", lambda e, h=h, sc=sc, kbf=kbf, kfm=kfm: e.matmul(
                    out=sc[:, 0:128], lhsT=kbf[:, h * 128:(h + 1) * 128],
                    rhs=kfm[:, h * 128:(h + 1) * 128], start=True, stop=True),
                    reads=[f"a_kbfm{par}", f"a_kfm{par}"], writes=[f"a_pb{hp}"])
                p.emit("tensor", lambda e, h=h, sc=sc, kbf=kbf, kfm=kfm: e.matmul(
                    out=sc[:, 128:256], lhsT=kfm[:, h * 128:(h + 1) * 128],
                    rhs=kbf[:, h * 128:(h + 1) * 128], start=True, stop=True),
                    reads=[f"a_kbfm{par}", f"a_kfm{par}"], writes=[f"a_pb{hp}"])
                p.emit("tensor", lambda e, h=h, sc=sc, kbf=kbf, qfm=qfm: e.matmul(
                    out=sc[:, 256:384], lhsT=kbf[:, h * 128:(h + 1) * 128],
                    rhs=qfm[:, h * 128:(h + 1) * 128], start=True, stop=True),
                    reads=[f"a_kbfm{par}", f"a_qfm{par}"], writes=[f"a_pb{hp}"])
                p.emit("tensor", lambda e, h=h, sc=sc, kfm=kfm: e.matmul(
                    out=sc[:, 384:512], lhsT=kfm[:, h * 128:(h + 1) * 128],
                    rhs=S_bf[:, h * 128:(h + 1) * 128], start=True, stop=True),
                    reads=[f"a_kfm{par}", "a_Sbf"], writes=[f"a_pb{hp}"])
                p.emit("tensor", lambda e, h=h, bc=bc: e.matmul(
                    out=bc[:, 0:128], lhsT=com.ones_rowf[0:1, :], rhs=LT8[h][:],
                    start=True, stop=True),
                    reads=[f"a_LT{h}", "c_ones_rowf"], writes=[f"a_pb{2+hp}"])
                p.emit("tensor", lambda e, h=h, bc=bc: e.matmul(
                    out=bc[:, 128:256], lhsT=com.ones_rowf[0:1, :], rhs=nLT8[h][:],
                    start=True, stop=True),
                    reads=[f"a_nLT{h}", "c_ones_rowf"], writes=[f"a_pb{2+hp}"])
                p.emit("vector", lambda e, h=h, bc=bc, Ltm=Ltm: e.tensor_scalar(
                    out=expTs[:, h * 128:(h + 1) * 128], in0=bc[:, 0:128],
                    scalar1=Ltm[:, h:h + 1], scalar2=0.0, op0=ALU.subtract, op1=ALU.min),
                    reads=[f"a_pb{2+hp}", f"a_Ltm{par}"], writes=["a_expTs"])
                p.emit("vector", lambda e, h=h, bc=bc, Ltm=Ltm: e.tensor_scalar(
                    out=expLs[:, h * 128:(h + 1) * 128], in0=bc[:, 128:256],
                    scalar1=Ltm[:, h:h + 1], scalar2=0.0, op0=ALU.add, op1=ALU.min),
                    reads=[f"a_pb{2+hp}", f"a_Ltm{par}"], writes=["a_expLs"])
                p.emit("vector", lambda e, h=h, sc=sc, ngm=ngm, vtm=vtm: e.scalar_tensor_tensor(
                    out=Xb[h][:], in0=sc[:, 384:512], scalar=ngm[:, h:h + 1],
                    in1=vtm[:, h * 128:(h + 1) * 128], op0=ALU.mult, op1=ALU.add),
                    reads=[f"a_pb{hp}", f"a_ngam{par}", f"a_vtm{par}"], writes=[f"a_X{h}"])
                p.emit("scalar", lambda e, h=h: e.activation(
                    out=expTs[:, h * 128:(h + 1) * 128],
                    in_=expTs[:, h * 128:(h + 1) * 128], func=AF.Exp),
                    reads=["a_expTs"], writes=["a_expTs"])
                p.emit("scalar", lambda e, h=h: e.activation(
                    out=expLs[:, h * 128:(h + 1) * 128],
                    in_=expLs[:, h * 128:(h + 1) * 128], func=AF.Exp),
                    reads=["a_expLs"], writes=["a_expLs"])
                p.emit("pool", lambda e, h=h: e.tensor_tensor(
                    out=GTs[:, h * 128:(h + 1) * 128], in0=expTs[:, h * 128:(h + 1) * 128],
                    in1=com.triu_s, op=ALU.mult),
                    reads=["a_expTs", "c_masks"], writes=["a_GTs"])
                p.emit("pool", lambda e, h=h: e.tensor_tensor(
                    out=GTi[:, h * 128:(h + 1) * 128], in0=expTs[:, h * 128:(h + 1) * 128],
                    in1=com.triu_i, op=ALU.mult),
                    reads=["a_expTs", "c_masks"], writes=["a_GTi"])
                p.emit("pool", lambda e, h=h: e.tensor_tensor(
                    out=GLs[:, h * 128:(h + 1) * 128], in0=expLs[:, h * 128:(h + 1) * 128],
                    in1=com.tril_s, op=ALU.mult),
                    reads=["a_expLs", "c_masks"], writes=["a_GLs"])
                p.emit("vector", lambda e, h=h, sc=sc: e.tensor_tensor(
                    out=M_T[h][:], in0=GTs[:, h * 128:(h + 1) * 128], in1=sc[:, 0:128],
                    op=ALU.mult),
                    reads=["a_GTs", f"a_pb{hp}"], writes=[f"a_MT{h}"])
                p.emit("vector", lambda e, h=h, sc=sc: e.tensor_tensor(
                    out=M_[h][:], in0=GLs[:, h * 128:(h + 1) * 128], in1=sc[:, 128:256],
                    op=ALU.mult),
                    reads=["a_GLs", f"a_pb{hp}"], writes=[f"a_M{h}"])
                p.emit("vector", lambda e, h=h, sc=sc: e.tensor_tensor(
                    out=A_T[h][:], in0=GTi[:, h * 128:(h + 1) * 128], in1=sc[:, 256:384],
                    op=ALU.mult),
                    reads=["a_GTi", f"a_pb{hp}"], writes=[f"a_AT{h}"])

            for h in range(H):
                sv = pb[4 + h % 2]
                p.emit("tensor", lambda e, h=h, sv=sv: e.matmul(
                    out=sv[:, 256:384], lhsT=M_T[h][:], rhs=Xb[h][:], start=True, stop=True),
                    reads=[f"a_MT{h}", f"a_X{h}"], writes=[f"a_pb{4+h%2}"])
                p.emit("vector", lambda e, h=h, sv=sv: e.tensor_tensor(
                    out=Xb[h][:], in0=Xb[h][:], in1=sv[:, 256:384], op=ALU.subtract),
                    reads=[f"a_X{h}", f"a_pb{4+h%2}"], writes=[f"a_X{h}"])

            srcP, srcPT = M_, M_T
            srct_, srctT = "a_M", "a_MT"
            for j in range(6):
                for h in range(H):
                    sv = pb[4 + h % 2]
                    p.emit("tensor", lambda e, h=h, sv=sv, srcP=srcP, srcPT=srcPT: e.matmul(
                        out=sv[:, 0:128], lhsT=srcP[h][:], rhs=srcPT[h][:],
                        start=True, stop=True),
                        reads=[f"{srct_}{h}", f"{srctT}{h}"], writes=[f"a_pb{4+h%2}"])
                    if j < 5:
                        p.emit("tensor", lambda e, h=h, sv=sv, srcP=srcP, srcPT=srcPT: e.matmul(
                            out=sv[:, 128:256], lhsT=srcPT[h][:], rhs=srcP[h][:],
                            start=True, stop=True),
                            reads=[f"{srct_}{h}", f"{srctT}{h}"], writes=[f"a_pb{4+h%2}"])
                    eng = "vector" if h % 2 == 0 else "scalar"
                    if eng == "vector":
                        p.emit("vector", lambda e, h=h, sv=sv: e.tensor_copy(
                            out=PwT[h][:], in_=sv[:, 0:128]),
                            reads=[f"a_pb{4+h%2}"], writes=[f"a_PT{h}"])
                        if j < 5:
                            p.emit("vector", lambda e, h=h, sv=sv: e.tensor_copy(
                                out=Pw[h][:], in_=sv[:, 128:256]),
                                reads=[f"a_pb{4+h%2}"], writes=[f"a_P{h}"])
                    else:
                        p.emit("scalar", lambda e, h=h, sv=sv: e.activation(
                            out=PwT[h][:], in_=sv[:, 0:128], func=AF.Identity),
                            reads=[f"a_pb{4+h%2}"], writes=[f"a_PT{h}"])
                        if j < 5:
                            p.emit("scalar", lambda e, h=h, sv=sv: e.activation(
                                out=Pw[h][:], in_=sv[:, 128:256], func=AF.Identity),
                                reads=[f"a_pb{4+h%2}"], writes=[f"a_P{h}"])
                for h in range(H):
                    sv = pb[4 + h % 2]
                    p.emit("tensor", lambda e, h=h, sv=sv: e.matmul(
                        out=sv[:, 256:384], lhsT=PwT[h][:], rhs=Xb[h][:],
                        start=True, stop=True),
                        reads=[f"a_PT{h}", f"a_X{h}"], writes=[f"a_pb{4+h%2}"])
                    p.emit("vector", lambda e, h=h, sv=sv: e.tensor_tensor(
                        out=Xb[h][:], in0=Xb[h][:], in1=sv[:, 256:384], op=ALU.add),
                        reads=[f"a_X{h}", f"a_pb{4+h%2}"], writes=[f"a_X{h}"])
                srcP, srcPT = Pw, PwT
                srct_, srctT = "a_P", "a_PT"

            for h in range(H):
                hp = h % 2
                bc = pb[2 + hp]
                sv = pb[4 + hp]
                p.emit("tensor", lambda e, h=h, bc=bc, qfm=qfm: e.matmul(
                    out=bc[:, 256:384], lhsT=qfm[:, h * 128:(h + 1) * 128],
                    rhs=S_bf[:, h * 128:(h + 1) * 128], start=True, stop=True),
                    reads=[f"a_qfm{par}", "a_Sbf"], writes=[f"a_pb{2+hp}"])
                p.emit("tensor", lambda e, h=h, bc=bc: e.matmul(
                    out=bc[:, 384:512], lhsT=A_T[h][:], rhs=Xb[h][:], start=True, stop=True),
                    reads=[f"a_AT{h}", f"a_X{h}"], writes=[f"a_pb{2+hp}"])
                p.emit("tensor", lambda e, h=h, sv=sv, kpm=kpm: e.matmul(
                    out=sv[:, 384:512], lhsT=kpm[:, h * 128:(h + 1) * 128], rhs=Xb[h][:],
                    start=True, stop=True),
                    reads=[f"a_kptm{par}", f"a_X{h}"], writes=[f"a_pb{4+hp}"])
                p.emit("vector", lambda e, h=h, bc=bc, gm=gm: e.tensor_scalar(
                    out=Og[:], in0=bc[:, 256:384], scalar1=gm[:, h:h + 1], scalar2=None,
                    op0=ALU.mult),
                    reads=[f"a_pb{2+hp}", f"a_gam{par}"], writes=["a_Og"])
                p.emit("vector", lambda e, h=h, bc=bc, otm=otm: e.tensor_tensor(
                    out=otm[:, h * 128:(h + 1) * 128], in0=Og[:], in1=bc[:, 384:512],
                    op=ALU.add),
                    reads=["a_Og", f"a_pb{2+hp}"], writes=[f"a_otm{par}"])
                p.emit("vector", lambda e, h=h, sv=sv, Ssrc=Ssrc, Sdst=Sdst, gCm=gCm:
                       e.scalar_tensor_tensor(
                    out=Sdst[:, h * 128:(h + 1) * 128], in0=Ssrc[:, h * 128:(h + 1) * 128],
                    scalar=gCm[:, h:h + 1], in1=sv[:, 384:512], op0=ALU.mult, op1=ALU.add),
                    reads=[stag_src, f"a_gC{par}", f"a_pb{4+hp}"], writes=[stag_dst])
            p.emit("vector", lambda e, Sdst=Sdst: e.tensor_copy(out=S_bf[:], in_=Sdst[:]),
                   reads=[stag_dst], writes=["a_Sbf"])
            tT(otm[:], f"a_otm{par}", o_fm[:], "a_ofm", par)
            for dh in range(2):
                bank = pb[dh]
                for c in range(DC):
                    p.emit("tensor", lambda e, c=c, dh=dh, bank=bank: e.matmul(
                        out=bank[:], lhsT=o_fm[:, c * 128:(c + 1) * 128],
                        rhs=wo[:, c * D + dh * 512: c * D + dh * 512 + 512],
                        start=(c == 0), stop=(c == DC - 1)),
                        reads=["a_ofm", "l0a_wo_bf"], writes=[f"a_pb{dh}"])
                p.emit("vector", lambda e, dh=dh, zb=zb, bank=bank: e.tensor_tensor(
                    out=zb[:, dh * 512:(dh + 1) * 512], in0=zb[:, dh * 512:(dh + 1) * 512],
                    in1=bank[:], op=ALU.add),
                    reads=[f"a_zt{par}", f"a_pb{dh}"], writes=[f"a_zt{par}"])
            p.dma(z1d[ci * 128:(ci + 1) * 128, :], zb[:], "az", reads=[f"a_zt{par}"])


    with p.scope("l0_attn_ffn"):
        w1 = p.sb("a_w1", [128, DC * 4096], BF16)
        w2 = p.sb("a_w2", [128, 32 * D], BF16)
        ln2 = p.sb("a_ln2", [128, 16], F32)
        b1fm = p.sb("a_b1fm", [128, 32], F32)
        b2rb = p.sb("a_b2rb", [1, D], BF16)
        with p.scope("l0_attn_ffn_wload"):
            fwld = [p.sb(f"a_fwld{i}", [128, 2048], F32) for i in range(2)]
            for c in range(DC):
                for hv in range(2):
                    b = fwld[hv]
                    p.dma(b[:], W["l0a_w1"][c * 128:(c + 1) * 128,
                                            hv * 2048:(hv + 1) * 2048], f"afw{hv}",
                          writes=[f"a_fwld{hv}"])
                    p.emit("vector", lambda e, b=b, c=c, hv=hv: e.tensor_copy(
                        out=w1[:, c * 4096 + hv * 2048: c * 4096 + hv * 2048 + 2048],
                        in_=b[:]),
                        reads=[f"a_fwld{hv}"], writes=["a_w1"])
            for c in range(32):
                b = fwld[c % 2]
                p.dma(b[:, 0:D], W["l0a_w2"][c * 128:(c + 1) * 128, :], f"afw{c%2}",
                      writes=[f"a_fwld{c%2}"])
                p.emit("vector", lambda e, b=b, c=c: e.tensor_copy(
                    out=w2[:, c * D:(c + 1) * D], in_=b[:, 0:D]),
                    reads=[f"a_fwld{c%2}"], writes=["a_w2"])
        p.dma(ln2[:], W["l0a_ln2"][:], "amisc", writes=["a_ln2"])
        p.dma(b1fm[:], W["l0a_b1fm"][:], "amisc", writes=["a_b1fm"])
        p.dma_fence("amisc")

        zt = [p.sb(f"a_fzt{i}", [128, D], F32) for i in range(2)]
        xt = [p.sb(f"a_fxt{i}", [128, D], F32) for i in range(2)]
        p.dma(zt[0][0:1, :], W["l0a_b2row"][:], "amisc", writes=["a_fzt0"])
        p.dma_fence("amisc")
        p.emit("vector", lambda e: e.tensor_copy(out=b2rb[:], in_=zt[0][0:1, :]),
               reads=["a_fzt0"], writes=["a_b2rb"])
        h_tm = p.sb("a_fhtm", [128, D], BF16)
        h_fm = p.sb("a_fhfm", [128, DC * 256], BF16)
        gbuf = p.sb("a_fgbuf", [128, 32 * 256], BF16)
        lnout = p.sb("a_flnout", [128, D], F32)
        pst = [p.ps(f"a_fpst{i}", [128, D], BF16) for i in range(2)]
        pz = [p.ps(f"a_fpz{i}", [128, 512], F32) for i in range(2)]
        po = [p.ps(f"a_fpo{i}", [128, 512], F32) for i in range(2)]

        for gi in range(T // 256):
            for i in range(2):
                ti = gi * 2 + i
                par = i % 2
                zb = zt[par]
                p.dma(zb[:], z1d[ti * 128:(ti + 1) * 128, :], f"afz{par}",
                      writes=[f"a_fzt{par}"])
                com.ln_tm(zb[:], f"a_fzt{par}", h_tm[:], "a_fhtm")
                ps_ap = pst[par]
                for c in range(DC):
                    p.emit("tensor", lambda e, c=c, ps_ap=ps_ap: e.transpose(
                        out=ps_ap[:, c * 128:(c + 1) * 128],
                        in_=h_tm[:, c * 128:(c + 1) * 128], identity=com.ident[:]),
                        reads=["a_fhtm", "c_ident"], writes=[f"a_fpst{par}"])
                for c in range(DC):
                    p.emit("vector", lambda e, c=c, i=i, ps_ap=ps_ap: e.tensor_scalar(
                        out=h_fm[:, c * 256 + i * 128: c * 256 + i * 128 + 128],
                        in0=ps_ap[:, c * 128:(c + 1) * 128],
                        scalar1=ln2[:, c:c + 1], scalar2=ln2[:, 8 + c:8 + c + 1],
                        op0=ALU.mult, op1=ALU.add),
                        reads=[f"a_fpst{par}", "a_ln2"], writes=["a_fhfm"])
            for fc in range(32):
                pzb = pz[fc % 2]
                for c in range(DC):
                    p.emit("tensor", lambda e, c=c, fc=fc, pzb=pzb: e.matmul(
                        out=pzb[:, 0:256], lhsT=w1[:, c * 4096 + fc * 128: c * 4096 + fc * 128 + 128],
                        rhs=h_fm[:, c * 256:(c + 1) * 256],
                        start=(c == 0), stop=(c == DC - 1)),
                        reads=["a_w1", "a_fhfm"], writes=[f"a_fpz{fc%2}"])
                p.emit("scalar", lambda e, fc=fc, pzb=pzb: e.activation(
                    out=gbuf[:, fc * 256:(fc + 1) * 256], in_=pzb[:, 0:256], func=AF.Gelu,
                    bias=b1fm[:, fc:fc + 1]),
                    reads=[f"a_fpz{fc%2}", "a_b1fm"], writes=[f"a_fg{fc}"])
            for i in range(2):
                ti = gi * 2 + i
                par = i % 2
                zb, xb = zt[par], xt[par]
                p.dma(zb[:], z1d[ti * 128:(ti + 1) * 128, :], f"afz{par}",
                      writes=[f"a_fzt{par}"])
                for dh in range(2):
                    pob = po[dh]
                    for fc in range(32):
                        p.emit("tensor", lambda e, fc=fc, i=i, dh=dh, pob=pob: e.matmul(
                            out=pob[:], lhsT=gbuf[:, fc * 256 + i * 128: fc * 256 + i * 128 + 128],
                            rhs=w2[:, fc * D + dh * 512: fc * D + dh * 512 + 512],
                            start=(fc == 0), stop=False),
                            reads=[f"a_fg{fc}", "a_w2"], writes=[f"a_fpo{dh}"])
                    p.emit("tensor", lambda e, dh=dh, pob=pob: e.matmul(
                        out=pob[:], lhsT=com.ones_row[0:1, :],
                        rhs=b2rb[0:1, dh * 512:dh * 512 + 512], start=False, stop=True),
                        reads=["a_b2rb", "c_ones_row"], writes=[f"a_fpo{dh}"])
                    p.emit("vector", lambda e, dh=dh, zb=zb, pob=pob: e.tensor_tensor(
                        out=zb[:, dh * 512:(dh + 1) * 512], in0=zb[:, dh * 512:(dh + 1) * 512],
                        in1=pob[:], op=ALU.add),
                        reads=[f"a_fzt{par}", f"a_fpo{dh}"], writes=[f"a_fzt{par}"])
                com.ln_tm(zb[:], f"a_fzt{par}", lnout[:], "a_flnout")
                p.dma(xb[:], xin[ti * 128:(ti + 1) * 128, :], f"afx{par}",
                      writes=[f"a_fxt{par}"])
                p.emit("vector", lambda e, xb=xb: e.tensor_tensor(
                    out=xb[:], in0=xb[:], in1=lnout[:], op=ALU.add),
                    reads=[f"a_fxt{par}", "a_flnout"], writes=[f"a_fxt{par}"])
                p.dma(xout[ti * 128:(ti + 1) * 128, :], xb[:], "axo",
                      reads=[f"a_fxt{par}"])



def emit_l1_attn(p, com, W, xin, xout, qd, od, T):
    H, HD, DL = 16, 64, 256
    NT = T // 128
    NQ = T // 512

    with p.scope("l1_attn"):
        latent = p.sb("t_lat", [128, 2 * T], BF16)
        Ep = p.sb("t_Ep", [128, T], BF16)
        Em = p.sb("t_Em", [128, T], BF16)
        Dcr = p.sb("t_Dcr", [128, 4 * 1024], BF16)
        onec = p.sb("t_onec", [128, 1], BF16)
        p.dma(Ep[:], W["l1a_ep"][:], "tmisc", writes=["t_Ep"])
        p.dma(Em[:], W["l1a_em"][:], "tmisc", writes=["t_Em"])
        p.dma(Dcr[:], W["l1a_dcross"][:], "tmisc", writes=["t_Dcr"])
        p.dma_fence("tmisc")
        p.emit("vector", lambda e: e.memset(onec[:], 1.0), writes=["t_onec"])

        with p.scope("l1_attn_b0"):
            wq = p.sb("t_wq", [128, DC * D], BF16)
            wk = p.sb("t_wk", [128, DC * D], BF16)
            wv = p.sb("t_wv", [128, DC * D], BF16)
            wdkv = p.sb("t_wdkv", [128, DC * DL], BF16)
            wldf = p.sb("t_wldf", [128, D], F32)
            for name, dst, nn in (("l1a_wq", wq, D), ("l1a_wk", wk, D),
                                  ("l1a_wv", wv, D), ("l1a_wdkv", wdkv, DL)):
                for c in range(DC):
                    p.dma(wldf[:, 0:nn], W[name][c * 128:(c + 1) * 128, :], "twl",
                          writes=["t_wldf"])
                    p.emit("vector", lambda e, c=c, dst=dst, nn=nn: e.tensor_copy(
                        out=dst[:, c * nn:(c + 1) * nn], in_=wldf[:, 0:nn]),
                        reads=["t_wldf"], writes=[f"{name}_bf"])
            xt = [p.sb(f"t_xt{i}", [128, D], F32) for i in range(2)]
            y_tm = p.sb("t_ytm", [128, D], BF16)
            y_fm = p.sb("t_yfm", [128, DC * 512], BF16)
            kv_fm = p.sb("t_kvfm", [128, DC * 512], BF16)
            qev = p.sb("t_qev", [128, 512], BF16)
            pst = p.ps("t_pst", [128, D], BF16)
            pq = [p.ps(f"t_pq{i}", [128, 512], F32) for i in range(2)]
            pk = [p.ps(f"t_pk{i}", [128, 512], F32) for i in range(2)]
            for gi in range(NQ):
                for i in range(4):
                    ti = gi * 4 + i
                    b = xt[i % 2]
                    p.dma(b[:], xin[ti * 128:(ti + 1) * 128, :], f"tx{i%2}",
                          writes=[f"t_xt{i%2}"])
                    com.ln_tm(b[:], f"t_xt{i%2}", y_tm[:], "t_ytm")
                    com.transpose_tile(y_tm[:], "t_ytm", pst[:], "t_pst",
                                       y_fm[:].rearrange("p (c t) -> p c t", c=DC)
                                       [:, :, i * 128:(i + 1) * 128], "t_yfm")
                for qc in range(DC):
                    pqb = pq[qc % 2]
                    for c in range(DC):
                        p.emit("tensor", lambda e, c=c, qc=qc, pqb=pqb: e.matmul(
                            out=pqb[:], lhsT=wq[:, c * D + qc * 128: c * D + qc * 128 + 128],
                            rhs=y_fm[:, c * 512:(c + 1) * 512],
                            start=(c == 0), stop=(c == DC - 1)),
                            reads=["t_yfm", "l1a_wq_bf"], writes=[f"t_pq{qc%2}"])
                    p.emit("vector", lambda e, qc=qc, pqb=pqb: e.tensor_copy(
                        out=qev[:], in_=pqb[:]),
                        reads=[f"t_pq{qc%2}"], writes=["t_qev"])
                    p.dma(qd[qc * 128:(qc + 1) * 128, gi * 512:(gi + 1) * 512],
                          qev[:], "tqo", reads=["t_qev"], eng="pool")
                for qc in range(DC):
                    pkb = pk[qc % 2]
                    for c in range(DC):
                        p.emit("tensor", lambda e, c=c, qc=qc, pkb=pkb: e.matmul(
                            out=pkb[:], lhsT=wk[:, c * D + qc * 128: c * D + qc * 128 + 128],
                            rhs=y_fm[:, c * 512:(c + 1) * 512],
                            start=(c == 0), stop=False),
                            reads=["t_yfm", "l1a_wk_bf"], writes=[f"t_pk{qc%2}"])
                        p.emit("tensor", lambda e, c=c, qc=qc, pkb=pkb: e.matmul(
                            out=pkb[:], lhsT=wv[:, c * D + qc * 128: c * D + qc * 128 + 128],
                            rhs=y_fm[:, c * 512:(c + 1) * 512],
                            start=False, stop=(c == DC - 1)),
                            reads=["t_yfm", "l1a_wv_bf"], writes=[f"t_pk{qc%2}"])
                    p.emit("vector", lambda e, qc=qc, pkb=pkb: e.tensor_copy(
                        out=kv_fm[:, qc * 512:(qc + 1) * 512], in_=pkb[:]),
                        reads=[f"t_pk{qc%2}"], writes=["t_kvfm"])
                for lc in range(2):
                    plb = pq[lc % 2]
                    for c in range(DC):
                        p.emit("tensor", lambda e, c=c, lc=lc, plb=plb: e.matmul(
                            out=plb[:, 0:512], lhsT=wdkv[:, c * DL + lc * 128: c * DL + lc * 128 + 128],
                            rhs=kv_fm[:, c * 512:(c + 1) * 512],
                            start=(c == 0), stop=(c == DC - 1)),
                            reads=["t_kvfm", "l1a_wdkv_bf"], writes=[f"t_pq{lc%2}"])
                    p.emit("vector", lambda e, lc=lc, gi=gi, plb=plb: e.tensor_copy(
                        out=latent[:, lc * T + gi * 512: lc * T + gi * 512 + 512],
                        in_=plb[:, 0:512]),
                        reads=[f"t_pq{lc%2}"], writes=["t_lat"])

        with p.scope("l1_attn_b1"):
            wupk = p.sb("t_wupk", [128, 2 * D], BF16)
            wupv = p.sb("t_wupv", [128, 2 * D], BF16)
            wldf = p.sb("t_wldf2", [128, D], F32)
            for name, dst in (("l1a_wupk", wupk), ("l1a_wupv", wupv)):
                for c in range(2):
                    p.dma(wldf[:], W[name][c * 128:(c + 1) * 128, :], "twl",
                          writes=["t_wldf2"])
                    p.emit("vector", lambda e, c=c, dst=dst: e.tensor_copy(
                        out=dst[:, c * D:(c + 1) * D], in_=wldf[:]),
                        reads=["t_wldf2"], writes=[f"{name}_bf"])
            kN = [p.sb(f"t_kN{i}", [128, T], BF16) for i in range(2)]
            kP = [p.sb(f"t_kP{i}", [128, T], BF16) for i in range(2)]
            kM = [p.sb(f"t_kM{i}", [128, T], BF16) for i in range(2)]
            vtok = [p.sb(f"t_vtok{i}", [128, 2 * NT * 65], BF16) for i in range(2)]
            qs = [p.sb(f"t_qs{i}", [128, T], BF16) for i in range(2)]
            qP = p.sb("t_qP", [128, 512], BF16)
            qM = p.sb("t_qM", [128, 512], BF16)
            Pt = [p.sb(f"t_P{i}", [128, 1024], BF16) for i in range(2)]
            Scr = [p.sb(f"t_Scr{i}", [128, 1024], F32) for i in range(2)]
            pOs = [p.sb(f"t_pOs{i}", [65, 512], BF16) for i in range(2)]
            otn = [p.sb(f"t_otn{i}", [128, 65], F32) for i in range(2)]
            oev = [p.sb(f"t_oev{i}", [128, 64], BF16) for i in range(2)]
            rden2 = [p.sb(f"t_rden{i}", [128, 1], F32) for i in range(2)]
            nsc = [p.sb(f"t_nsc{i}", [128, 8], F32) for i in range(2)]
            mh = [p.sb(f"t_mh{i}", [128, 1], F32) for i in range(2)]
            psW = [p.ps(f"t_psW{i}", [128, 1024], F32) for i in range(2)]
            psO = [p.ps(f"t_psO{i}", [128, 512], F32) for i in range(2)]
            pstb = [p.ps(f"t_pstb{i}", [128, D], BF16) for i in range(2)]
            for i in range(2):
                p.emit("vector", lambda e, i=i: e.memset(vtok[i][:], 0.0),
                       writes=[f"t_vtok{i}"])
                for ti in range(2 * NT):
                    p.emit("vector", lambda e, ti=ti, i=i: e.memset(
                        vtok[i][:, ti * 65 + 64: ti * 65 + 65], 1.0),
                        writes=[f"t_vtok{i}"])

            for hp2 in range(H // 2):
                hA, hB = 2 * hp2, 2 * hp2 + 1
                hq = hp2 % 2
                kNh, kPh, kMh, vth = kN[hq], kP[hq], kM[hq], vtok[hq]
                nsch, mhh = nsc[hq], mh[hq]
                p.dma(qs[hq][0:64, :], qd[hA * 64:(hA + 1) * 64, :], f"tq{hq}",
                      writes=[f"t_qs{hq}"])
                p.dma(qs[hq][64:128, :], qd[hB * 64:(hB + 1) * 64, :], f"tq{hq}",
                      writes=[f"t_qs{hq}"])
                for s5 in range(NQ):
                    psk = psW[s5 % 2]
                    for lc in range(2):
                        p.emit("tensor", lambda e, hA=hA, s5=s5, lc=lc, psk=psk: e.matmul(
                            out=psk[0:64, 0:512],
                            lhsT=wupk[:, lc * D + hA * 64: lc * D + hA * 64 + 64],
                            rhs=latent[:, lc * T + s5 * 512: lc * T + s5 * 512 + 512],
                            start=(lc == 0), stop=(lc == 1)),
                            reads=["l1a_wupk_bf", "t_lat"], writes=[f"t_psW{s5%2}"])
                    for lc in range(2):
                        p.emit("tensor", lambda e, hB=hB, s5=s5, lc=lc, psk=psk: e.matmul(
                            out=psk[64:128, 0:512],
                            lhsT=wupk[:, lc * D + hB * 64: lc * D + hB * 64 + 64],
                            rhs=latent[:, lc * T + s5 * 512: lc * T + s5 * 512 + 512],
                            start=(lc == 0), stop=(lc == 1),
                            tile_position=(0, 64)),
                            reads=["l1a_wupk_bf", "t_lat"], writes=[f"t_psW{s5%2}"])
                    p.emit("vector", lambda e, s5=s5, psk=psk, kNh=kNh: e.tensor_copy(
                        out=kNh[:, s5 * 512:(s5 + 1) * 512], in_=psk[:, 0:512]),
                        reads=[f"t_psW{s5%2}"], writes=[f"t_kN{hq}"])
                    p.emit("vector", lambda e, s5=s5, psk=psk, kPh=kPh: e.tensor_tensor(
                        out=kPh[:, s5 * 512:(s5 + 1) * 512], in0=psk[:, 0:512],
                        in1=Ep[:, s5 * 512:(s5 + 1) * 512], op=ALU.mult),
                        reads=[f"t_psW{s5%2}", "t_Ep"], writes=[f"t_kP{hq}"])
                    p.emit("vector", lambda e, s5=s5, psk=psk, kMh=kMh: e.tensor_tensor(
                        out=kMh[:, s5 * 512:(s5 + 1) * 512], in0=psk[:, 0:512],
                        in1=Em[:, s5 * 512:(s5 + 1) * 512], op=ALU.mult),
                        reads=[f"t_psW{s5%2}", "t_Em"], writes=[f"t_kM{hq}"])
                for hX, colbase in ((hA, 0), (hB, NT * 65)):
                    for ti in range(NT):
                        psv = psO[ti % 2]
                        for lc in range(2):
                            p.emit("tensor", lambda e, hX=hX, ti=ti, lc=lc, psv=psv: e.matmul(
                                out=psv[:, 0:64],
                                lhsT=latent[:, lc * T + ti * 128: lc * T + ti * 128 + 128],
                                rhs=wupv[:, lc * D + hX * 64: lc * D + hX * 64 + 64],
                                start=(lc == 0), stop=(lc == 1)),
                                reads=["l1a_wupv_bf", "t_lat"], writes=[f"t_psO{ti%2}"])
                        p.emit("vector", lambda e, ti=ti, psv=psv, vth=vth, colbase=colbase:
                               e.tensor_copy(
                            out=vth[:, colbase + ti * 65: colbase + ti * 65 + 64],
                            in_=psv[:, 0:64]),
                            reads=[f"t_psO{ti%2}"], writes=[f"t_vtok{hq}"])
                p.emit("vector", lambda e, hq=hq, nsch=nsch: e.reduce_max(
                    out=nsch[:, 0:1], in_=qs[hq][:], axis=AX.X, apply_absolute_value=True),
                    reads=[f"t_qs{hq}"], writes=[f"t_nsc{hq}"])
                p.emit("vector", lambda e, kNh=kNh, nsch=nsch: e.reduce_max(
                    out=nsch[:, 1:2], in_=kNh[:], axis=AX.X, apply_absolute_value=True),
                    reads=[f"t_kN{hq}", f"t_nsc{hq}"], writes=[f"t_nsc{hq}"])
                p.emit("vector", lambda e, nsch=nsch: e.tensor_tensor(
                    out=nsch[:, 2:3], in0=nsch[:, 0:1], in1=nsch[:, 1:2], op=ALU.mult),
                    reads=[f"t_nsc{hq}"], writes=[f"t_nsc{hq}"])
                p.emit("tensor", lambda e, nsch=nsch: e.transpose(
                    out=p.psfw[0:1, 0:128], in_=nsch[:, 2:3],
                    identity=com.idf32[:]),
                    reads=[f"t_nsc{hq}", "c_idf32"], writes=["t_psW0"])
                p.emit("vector", lambda e, nsch=nsch: e.reduce_max(
                    out=nsch[0:1, 3:4], in_=p.psfw[0:1, 0:128], axis=AX.X),
                    reads=["t_psW0"], writes=[f"t_nsc{hq}b"])
                p.emit("vector", lambda e, nsch=nsch: e.tensor_scalar(
                    out=nsch[0:1, 4:5], in0=nsch[0:1, 3:4], scalar1=-1.5, scalar2=None,
                    op0=ALU.mult),
                    reads=[f"t_nsc{hq}b"], writes=[f"t_nsc{hq}b"])
                p.emit("tensor", lambda e, nsch=nsch: e.matmul(
                    out=p.psfw[:, 64:65], lhsT=com.ones_rowf[0:1, :], rhs=nsch[0:1, 4:5],
                    start=True, stop=True),
                    reads=[f"t_nsc{hq}b", "c_ones_rowf"], writes=["t_psW0"])
                p.emit("vector", lambda e, mhh=mhh: e.tensor_copy(
                    out=mhh[:], in_=p.psfw[:, 64:65]),
                    reads=["t_psW0"], writes=[f"t_mh{hq}"])

                for qi in range(NQ):
                    p.emit("vector", lambda e, qi=qi, hq=hq: e.tensor_tensor(
                        out=qP[:], in0=qs[hq][:, qi * 512:(qi + 1) * 512],
                        in1=Ep[:, qi * 512:(qi + 1) * 512], op=ALU.mult),
                        reads=[f"t_qs{hq}", "t_Ep"], writes=["t_qP"])
                    p.emit("vector", lambda e, qi=qi, hq=hq: e.tensor_tensor(
                        out=qM[:], in0=qs[hq][:, qi * 512:(qi + 1) * 512],
                        in1=Em[:, qi * 512:(qi + 1) * 512], op=ALU.mult),
                        reads=[f"t_qs{hq}", "t_Em"], writes=["t_qM"])

                    def emit_scores(ki):
                        ks = psW[ki % 2]
                        if ki * 128 + 128 <= qi * 512:
                            lh, lht, rh, rht = kPh, f"t_kP{hq}", qM, "t_qM"
                        elif ki * 128 >= qi * 512 + 512:
                            lh, lht, rh, rht = kMh, f"t_kM{hq}", qP, "t_qP"
                        else:
                            lh, lht, rh, rht = kNh, f"t_kN{hq}", qs[hq], f"t_qs{hq}"
                        for half, r0 in ((0, 0), (1, 64)):
                            rsl = (rh[r0:r0 + 64, qi * 512:(qi + 1) * 512]
                                   if rht == f"t_qs{hq}" else rh[r0:r0 + 64, :])
                            p.emit("tensor", lambda e, ki=ki, ks=ks, half=half, lh=lh,
                                   r0=r0, rsl=rsl: e.matmul(
                                out=ks[:, half * 512:(half + 1) * 512],
                                lhsT=lh[r0:r0 + 64, ki * 128:(ki + 1) * 128],
                                rhs=rsl, start=True, stop=True),
                                reads=[lht, rht], writes=[f"t_psW{ki%2}"])

                    emit_scores(0)
                    for ki in range(NT):
                        ks = psW[ki % 2]
                        Ptb = Pt[ki % 2]
                        crossing = (qi * 4 <= ki < qi * 4 + 4)
                        if crossing:
                            dcol = (ki - qi * 4) * 1024
                            p.emit("vector", lambda e, ks=ks, ki=ki, dcol=dcol:
                                   e.tensor_tensor(
                                out=Scr[ki % 2][:], in0=ks[:],
                                in1=Dcr[:, dcol:dcol + 1024], op=ALU.mult),
                                reads=[f"t_psW{ki%2}", "t_Dcr"],
                                writes=[f"t_Scr{ki%2}"])
                            p.emit("scalar", lambda e, ki=ki, Ptb=Ptb, mhh=mhh:
                                   e.activation(
                                out=Ptb[:], in_=Scr[ki % 2][:], func=AF.Exp, bias=mhh[:]),
                                reads=[f"t_Scr{ki%2}", f"t_mh{hq}"],
                                writes=[f"t_Pt{ki%2}"])
                        else:
                            p.emit("scalar", lambda e, ks=ks, Ptb=Ptb, ki=ki, mhh=mhh:
                                   e.activation(
                                out=Ptb[:], in_=ks[:], func=AF.Exp, bias=mhh[:]),
                                reads=[f"t_psW{ki%2}", f"t_mh{hq}"],
                                writes=[f"t_Pt{ki%2}"])
                        if ki + 1 < NT:
                            emit_scores(ki + 1)
                        for half, colbase in ((0, 0), (1, NT * 65)):
                            p.emit("tensor", lambda e, ki=ki, half=half, colbase=colbase,
                                   Ptb=Ptb, vth=vth: e.matmul(
                                out=psO[half][0:65, :],
                                lhsT=vth[:, colbase + ki * 65: colbase + ki * 65 + 65],
                                rhs=Ptb[:, half * 512:(half + 1) * 512],
                                start=(ki == 0), stop=(ki == NT - 1)),
                                reads=[f"t_vtok{hq}", f"t_Pt{ki%2}"],
                                writes=[f"t_psO{half}"])
                    for half, hX in ((0, hA), (1, hB)):
                        p.emit("vector", lambda e, half=half: e.tensor_copy(
                            out=pOs[half][:], in_=psO[half][0:65, :]),
                            reads=[f"t_psO{half}"], writes=[f"t_pOs{half}"])
                        for sub in range(4):
                            sp = half
                            p.emit("tensor", lambda e, sub=sub, half=half, sp=sp: e.transpose(
                                out=pstb[sp][:, 0:65],
                                in_=pOs[half][0:65, sub * 128:(sub + 1) * 128],
                                identity=com.ident[0:65, 0:65]),
                                reads=[f"t_pOs{half}", "c_ident"], writes=[f"t_pstb{sp}"])
                            p.emit("vector", lambda e, sp=sp: e.tensor_copy(
                                out=otn[sp][:], in_=pstb[sp][:, 0:65]),
                                reads=[f"t_pstb{sp}"], writes=[f"t_otn{sp}"])
                            p.emit("vector", lambda e, sp=sp: e.reciprocal(
                                out=rden2[sp][:], in_=otn[sp][:, 64:65]),
                                reads=[f"t_otn{sp}"], writes=[f"t_rden{sp}"])
                            p.emit("vector", lambda e, sp=sp: e.tensor_scalar(
                                out=oev[sp][:], in0=otn[sp][:, 0:64], scalar1=rden2[sp][:, 0:1],
                                scalar2=None, op0=ALU.mult),
                                reads=[f"t_otn{sp}", f"t_rden{sp}"], writes=[f"t_oev{sp}"])
                            p.dma(od[(qi * 4 + sub) * 128:(qi * 4 + sub) * 128 + 128,
                                     hX * 64:(hX + 1) * 64],
                                  oev[sp][:], "too", reads=[f"t_oev{sp}"], eng="pool")

        with p.scope("l1_attn_b2"):
            wo = p.sb("t_wo", [128, DC * D], BF16)
            wldf = p.sb("t_wldf3", [128, D], F32)
            for c in range(DC):
                p.dma(wldf[:], W["l1a_wo"][c * 128:(c + 1) * 128, :], "twl",
                      writes=["t_wldf3"])
                p.emit("vector", lambda e, c=c: e.tensor_copy(
                    out=wo[:, c * D:(c + 1) * D], in_=wldf[:]),
                    reads=["t_wldf3"], writes=["t_wo_bf"])
            ot = [p.sb(f"t_ot{i}", [128, D], BF16) for i in range(2)]
            o_fm = p.sb("t_ofm", [128, D], BF16)
            zt = [p.sb(f"t_zt{i}", [128, D], F32) for i in range(2)]
            xt = [p.sb(f"t_xt2{i}", [128, D], F32) for i in range(2)]
            lnout = p.sb("t_lnout", [128, D], F32)
            pst = p.ps("t_pst2", [128, D], BF16)
            pz = [p.ps(f"t_pz{i}", [128, 512], F32) for i in range(2)]
            for ti in range(NT):
                par = ti % 2
                ob, zb, xb = ot[par], zt[par], xt[par]
                p.dma(ob[:], od[ti * 128:(ti + 1) * 128, :], f"tol{par}",
                      writes=[f"t_ot{par}"])
                for c in range(DC):
                    p.emit("tensor", lambda e, c=c, ob=ob: e.transpose(
                        out=pst[:, c * 128:(c + 1) * 128], in_=ob[:, c * 128:(c + 1) * 128],
                        identity=com.ident[:]),
                        reads=[f"t_ot{par}", "c_ident"], writes=["t_pst2"])
                p.emit("vector", lambda e: e.tensor_copy(
                    out=o_fm[:].rearrange("p (c f) -> p c f", c=DC),
                    in_=pst[:].rearrange("p (c f) -> p c f", c=DC)),
                    reads=["t_pst2"], writes=["t_ofm"])
                for dh in range(2):
                    pzb = pz[dh]
                    for c in range(DC):
                        p.emit("tensor", lambda e, c=c, dh=dh, pzb=pzb: e.matmul(
                            out=pzb[:], lhsT=o_fm[:, c * 128:(c + 1) * 128],
                            rhs=wo[:, c * D + dh * 512: c * D + dh * 512 + 512],
                            start=(c == 0), stop=(c == DC - 1)),
                            reads=["t_ofm", "t_wo_bf"], writes=[f"t_pz{dh}"])
                    p.emit("vector", lambda e, dh=dh, zb=zb, pzb=pzb: e.tensor_copy(
                        out=zb[:, dh * 512:(dh + 1) * 512], in_=pzb[:]),
                        reads=[f"t_pz{dh}"], writes=[f"t_zt{par}"])
                com.ln_tm(zb[:], f"t_zt{par}", lnout[:], "t_lnout")
                p.dma(xb[:], xin[ti * 128:(ti + 1) * 128, :], f"tx2{par}",
                      writes=[f"t_xt2{par}"])
                p.emit("vector", lambda e, xb=xb: e.tensor_tensor(
                    out=xb[:], in0=xb[:], in1=lnout[:], op=ALU.add),
                    reads=[f"t_xt2{par}", "t_lnout"], writes=[f"t_xt2{par}"])
                p.dma(xout[ti * 128:(ti + 1) * 128, :], xb[:], "txo",
                      reads=[f"t_xt2{par}"])



def emit_l1_ffn(p, com, W, xin, xout, y3fm, T):
    F = 4096
    FC = F // 128
    NG = T // 512

    with p.scope("l1_ffn"):
        mb1 = p.sb("g_mb1", [128, FC], F32)
        mb2r = p.sb("g_mb2r", [1, D], BF16)
        g_bc = p.sb("g_gbc", [128, 8], F32)
        gcol_b = p.sb("g_gcolb", [8, 1], BF16)

        with p.scope("l1_ffn_gate"):
            xt = [p.sb(f"g_xt{i}", [128, D], F32) for i in range(2)]
            y_tm = p.sb("g_ytm", [128, D], BF16)
            y_fm = p.sb("g_yfm", [128, DC * 512], BF16)
            gw1 = p.sb("g_gw1", [128, DC * D], BF16)
            gw2 = p.sb("g_gw2", [128, 64], BF16)
            gb1 = p.sb("g_gb1", [128, 8], F32)
            gb2 = p.sb("g_gb2", [8, 1], F32)
            g1 = p.sb("g_g1", [128, 512], BF16)
            g2s = p.sb("g_g2s", [8, 4], F32)
            grow = p.sb("g_grow", [1, 8], BF16)
            gwf = p.sb("g_gwf", [128, D], F32)
            ps_t = p.ps("g_pst", [128, D], BF16)
            pga = [p.ps(f"g_pga{i}", [128, 512], F32) for i in range(2)]
            pg2 = p.ps("g_pg2", [8, 512], F32)
            pr = p.ps("g_pr", [128, 512], F32)

            for c in range(DC):
                p.dma(gwf[:], W["l1g_w1t"][c * 128:(c + 1) * 128, :], f"gw{c%2}",
                      writes=["g_gwf"])
                p.emit("vector", lambda e, c=c: e.tensor_copy(
                    out=gw1[:, c * D:(c + 1) * D], in_=gwf[:]),
                    reads=["g_gwf"], writes=["g_gw1"])
            p.dma(gwf[:, 0:64].rearrange("p (c f) -> p c f", f=8),
                  W["l1g_w2t"][:, :].rearrange("(c p) f -> p c f", p=128), "gmisc",
                  writes=["g_gwf"])
            p.dma(gb1[:], W["l1g_b1fm"][:], "gmisc", writes=["g_gb1"])
            p.dma(gb2[:], W["l1g_b2col"][:], "gmisc", writes=["g_gb2"])
            p.dma_fence("gmisc")
            p.emit("vector", lambda e: e.tensor_copy(out=gw2[:], in_=gwf[:, 0:64]),
                   reads=["g_gwf"], writes=["g_gw2"])

            for gi in range(NG):
                for i in range(4):
                    ti = gi * 4 + i
                    b = xt[i % 2]
                    p.dma(b[:], xin[ti * 128:(ti + 1) * 128, :], f"gx{i%2}",
                          writes=[f"g_xt{i%2}"])
                    com.ln_tm(b[:], f"g_xt{i%2}", y_tm[:], "g_ytm")
                    com.transpose_tile(y_tm[:], "g_ytm", ps_t[:], "g_pst",
                                       y_fm[:].rearrange("p (c t) -> p c t", c=DC)
                                       [:, :, i * 128:(i + 1) * 128], "g_yfm")
                    p.dma(y3fm[:, :].rearrange("(c p) t -> p c t", p=128)
                          [:, :, ti * 128:(ti + 1) * 128],
                          y_fm[:].rearrange("p (c t) -> p c t", c=DC)
                          [:, :, i * 128:(i + 1) * 128],
                          "gy3o", reads=["g_yfm"], eng="pool")
                for oc in range(DC):
                    pga_ = pga[oc % 2]
                    for c in range(DC):
                        p.emit("tensor", lambda e, c=c, oc=oc, pga_=pga_: e.matmul(
                            out=pga_[:], lhsT=gw1[:, c * D + oc * 128: c * D + oc * 128 + 128],
                            rhs=y_fm[:, c * 512:(c + 1) * 512],
                            start=(c == 0), stop=(c == DC - 1)),
                            reads=["g_gw1", "g_yfm"], writes=[f"g_pga{oc%2}"])
                    p.emit("scalar", lambda e, oc=oc, pga_=pga_: e.activation(
                        out=g1[:], in_=pga_[:], func=AF.Relu, bias=gb1[:, oc:oc + 1]),
                        reads=[f"g_pga{oc%2}", "g_gb1"], writes=["g_g1"])
                    p.emit("tensor", lambda e, oc=oc: e.matmul(
                        out=pg2[:], lhsT=gw2[:, oc * 8:oc * 8 + 8],
                        rhs=g1[:], start=(oc == 0), stop=(oc == DC - 1)),
                        reads=["g_gw2", "g_g1"], writes=["g_pg2"])
                p.emit("vector", lambda e: e.reduce_sum(
                    out=g2s[:, 1:2], in_=pg2[:], axis=AX.X),
                    reads=["g_pg2"], writes=["g_g2p"])
                if gi == 0:
                    p.emit("vector", lambda e: e.tensor_copy(out=g2s[:, 0:1], in_=g2s[:, 1:2]),
                           reads=["g_g2p"], writes=["g_g2a"])
                else:
                    p.emit("vector", lambda e: e.tensor_tensor(
                        out=g2s[:, 0:1], in0=g2s[:, 0:1], in1=g2s[:, 1:2], op=ALU.add),
                        reads=["g_g2a", "g_g2p"], writes=["g_g2a"])
            p.emit("vector", lambda e: e.tensor_scalar(
                out=g2s[:, 2:3], in0=g2s[:, 0:1], scalar1=1.0 / T, scalar2=None, op0=ALU.mult),
                reads=["g_g2a"], writes=["g_g2a"])
            p.emit("vector", lambda e: e.tensor_tensor(
                out=g2s[:, 3:4], in0=g2s[:, 2:3], in1=gb2[:, 0:1], op=ALU.add),
                reads=["g_g2a", "g_gb2"], writes=["g_gcol"])
            p.emit("vector", lambda e: e.tensor_copy(out=gcol_b[:], in_=g2s[:, 3:4]),
                   reads=["g_gcol"], writes=["g_gcolb"])
            p.emit("tensor", lambda e: e.matmul(out=pr[0:1, 0:8], lhsT=gcol_b[:],
                                                rhs=com.ident[0:8, 0:8], start=True, stop=True),
                   reads=["g_gcolb", "c_ident"], writes=["g_pr"])
            p.emit("vector", lambda e: e.tensor_copy(out=grow[:], in_=pr[0:1, 0:8]),
                   reads=["g_pr"], writes=["g_grow"])
            p.emit("tensor", lambda e: e.matmul(out=pr[:, 0:8], lhsT=com.ones_row[0:1, :],
                                                rhs=grow[:], start=True, stop=True),
                   reads=["g_grow", "c_ones_row"], writes=["g_pr"])
            p.emit("vector", lambda e: e.tensor_copy(out=g_bc[:], in_=pr[:, 0:8]),
                   reads=["g_pr"], writes=["g_gbc"])

        mW1 = p.sb("g_mw1", [128, DC * F], BF16)
        mW2 = p.sb("g_mw2", [128, FC * D], BF16)
        with p.scope("l1_ffn_W"):
            accV = p.sb("g_accv", [128, 2 * 1024], F32)
            accP = p.sb("g_accp", [128, 1024], F32)
            twb = [p.sb(f"g_twb{i}", [128, F // 4], F32) for i in range(4)]
            tbf = p.sb("g_tbf", [128, 8 * FC], F32)
            tb2b = p.sb("g_tb2b", [8, D], BF16)
            bb = p.sb("g_b2rb", [1, D], BF16)
            pr2 = p.ps("g_pr2", [128, 512], F32)

            p.dma(accV[0:1, 0:D], W["l1f_b2row"][:], "gmisc", writes=["g_accA"])
            p.dma(accV[0:8, D:2 * D], W["l1f_tb2"][:], "gmisc", writes=["g_accB"])
            p.dma(mb1[:], W["l1f_b1fm"][:], "gmisc", writes=["g_mb1"])
            p.dma(tbf[:], W["l1f_tb1fm"][:], "gmisc", writes=["g_tbf"])
            p.dma_fence("gmisc")
            p.emit("vector", lambda e: e.tensor_copy(out=bb[:], in_=accV[0:1, 0:D]),
                   reads=["g_accA"], writes=["g_bb"])
            p.emit("vector", lambda e: e.tensor_copy(out=tb2b[:], in_=accV[0:8, D:2 * D]),
                   reads=["g_accB"], writes=["g_tb2b"])
            for e8 in range(8):
                p.emit("vector", lambda e, e8=e8: e.scalar_tensor_tensor(
                    out=mb1[:], in0=tbf[:, e8 * FC:(e8 + 1) * FC], scalar=g_bc[:, e8:e8 + 1],
                    in1=mb1[:], op0=ALU.mult, op1=ALU.add),
                    reads=["g_tbf", "g_mb1"], writes=["g_mb1"])
            j = 0
            vg = 0
            for ch in range(DC):
                for hv in range(4):
                    if hv < 3:
                        r = vg % 2
                        vg += 1
                        ro = r * 1024
                        tv = f"g_acc{'AB'[r]}"
                        ar_t, ar_off = accV, ro
                        p.dma(accV[:, ro:ro + 1024],
                              W["l1f_w1t"][ch * 128:(ch + 1) * 128,
                                           hv * 1024:(hv + 1) * 1024],
                              "gw1l", writes=[tv])
                    else:
                        tv = "g_accP"
                        ar_t, ar_off = accP, 0
                        p.dma(accP[:], W["l1f_w1t"][ch * 128:(ch + 1) * 128,
                                                    3 * 1024:4 * 1024],
                              "gw1l", writes=[tv])
                    for e8 in range(8):
                        sl = j % 4
                        j += 1
                        tb = twb[sl]
                        p.dma(tb[:], W["l1f_tw1t"][e8, ch * 128:(ch + 1) * 128,
                                                   hv * 1024:(hv + 1) * 1024],
                              f"gtw{sl}", writes=[f"g_twb{sl}"],
                              eng="sync" if sl % 2 == 0 else "scalar")
                        p.emit("vector", lambda e, e8=e8, tb=tb, ar_t=ar_t,
                               ar_off=ar_off: e.scalar_tensor_tensor(
                            out=ar_t[:, ar_off:ar_off + 1024], in0=tb[:],
                            scalar=g_bc[:, e8:e8 + 1],
                            in1=ar_t[:, ar_off:ar_off + 1024],
                            op0=ALU.mult, op1=ALU.add),
                            reads=[f"g_twb{sl}", tv], writes=[tv])
                    p.emit("pool", lambda e, ch=ch, hv=hv, ar_t=ar_t,
                           ar_off=ar_off: e.tensor_copy(
                        out=mW1[:, ch * F + hv * 1024:ch * F + (hv + 1) * 1024],
                        in_=ar_t[:, ar_off:ar_off + 1024]),
                        reads=[tv], writes=[f"g_mw1_{ch}"])
            for fc in range(FC):
                r = fc % 3
                tv = ("g_accA", "g_accB", "g_accP")[r]
                ar_t, ar_off = (accV, accV, accP)[r], (0, 1024, 0)[r]
                p.dma(ar_t[:, ar_off:ar_off + D],
                      W["l1f_w2t"][fc * 128:(fc + 1) * 128, :],
                      "gw1l", writes=[tv])
                for e8 in range(8):
                    sl = j % 4
                    j += 1
                    tb = twb[sl]
                    p.dma(tb[:, 0:D], W["l1f_tw2t"][e8, fc * 128:(fc + 1) * 128, :],
                          f"gtw{sl}", writes=[f"g_twb{sl}"],
                          eng="sync" if sl % 2 == 0 else "scalar")
                    p.emit("vector", lambda e, e8=e8, tb=tb, ar_t=ar_t,
                           ar_off=ar_off: e.scalar_tensor_tensor(
                        out=ar_t[:, ar_off:ar_off + D], in0=tb[:, 0:D],
                        scalar=g_bc[:, e8:e8 + 1],
                        in1=ar_t[:, ar_off:ar_off + D], op0=ALU.mult, op1=ALU.add),
                        reads=[f"g_twb{sl}", tv], writes=[tv])
                p.emit("pool", lambda e, fc=fc, ar_t=ar_t, ar_off=ar_off: e.tensor_copy(
                    out=mW2[:, fc * D:(fc + 1) * D], in_=ar_t[:, ar_off:ar_off + D]),
                    reads=[tv], writes=[f"g_mw2_{fc}"])
            for dh in range(2):
                p.emit("tensor", lambda e, dh=dh: e.matmul(
                    out=pr2[0:1, 0:512], lhsT=gcol_b[:],
                    rhs=tb2b[:, dh * 512:dh * 512 + 512], start=True, stop=False),
                    reads=["g_tb2b"], writes=["g_pr2"])
                p.emit("tensor", lambda e, dh=dh: e.matmul(
                    out=pr2[0:1, 0:512], lhsT=com.ones_row[0:1, 0:1],
                    rhs=bb[:, dh * 512:dh * 512 + 512], start=False, stop=True),
                    reads=["g_bb", "c_ones_row"], writes=["g_pr2"])
                p.emit("vector", lambda e, dh=dh: e.tensor_copy(
                    out=mb2r[0:1, dh * 512:dh * 512 + 512], in_=pr2[0:1, 0:512]),
                    reads=["g_pr2"], writes=["g_mb2r"])

        with p.scope("l1_ffn_UO"):
            QU = 256
            NGU = T // QU
            yslab = p.sb("g_yslab", [128, DC * QU], BF16)
            ub = p.sb("g_ub", [128, FC * QU], BF16)
            xt = [p.sb(f"g_xt{i}", [128, D], F32) for i in range(2)]
            ot = [p.sb(f"g_ot{i}", [128, D], F32) for i in range(2)]
            lnout = p.sb("g_lnout", [128, D], F32)
            pu_ = [p.ps(f"g_pu{i}", [128, QU], F32) for i in range(2)]
            po_ = [p.ps(f"g_po{i}", [128, 512], F32) for i in range(2)]
            for gi in range(NGU):
                p.dma(yslab[:].rearrange("p (c t) -> p c t", c=DC),
                      y3fm[:, gi * QU:(gi + 1) * QU].rearrange("(c p) t -> p c t", p=128),
                      f"gys{gi%2}", writes=["g_yslab"])
                for fc in range(FC):
                    pu1 = pu_[fc % 2]
                    for ch in range(DC):
                        p.emit("tensor", lambda e, fc=fc, ch=ch, pu1=pu1: e.matmul(
                            out=pu1[:], lhsT=mW1[:, ch * F + fc * 128: ch * F + fc * 128 + 128],
                            rhs=yslab[:, ch * QU:(ch + 1) * QU],
                            start=(ch == 0), stop=(ch == DC - 1)),
                            reads=[f"g_mw1_{ch}", "g_yslab"], writes=[f"g_pu{fc%2}"])
                    p.emit("scalar", lambda e, fc=fc, pu1=pu1: e.activation(
                        out=ub[:, fc * QU:(fc + 1) * QU], in_=pu1[:],
                        func=AF.Gelu_apprx_tanh, bias=mb1[:, fc:fc + 1]),
                        reads=[f"g_pu{fc%2}", "g_mb1"], writes=[f"g_ub{fc}"])
                for i in range(QU // 128):
                    ti = gi * (QU // 128) + i
                    ob = ot[i % 2]
                    for dh in range(2):
                        po1 = po_[dh]
                        for fc in range(FC):
                            p.emit("tensor", lambda e, fc=fc, i=i, dh=dh, po1=po1: e.matmul(
                                out=po1[:], lhsT=ub[:, fc * QU + i * 128: fc * QU + i * 128 + 128],
                                rhs=mW2[:, fc * D + dh * 512: fc * D + dh * 512 + 512],
                                start=(fc == 0), stop=False),
                                reads=[f"g_ub{fc}", f"g_mw2_{fc}"], writes=[f"g_po{dh}"])
                        p.emit("tensor", lambda e, dh=dh, po1=po1: e.matmul(
                            out=po1[:], lhsT=com.ones_row[0:1, :],
                            rhs=mb2r[0:1, dh * 512:dh * 512 + 512], start=False, stop=True),
                            reads=["g_mb2r", "c_ones_row"], writes=[f"g_po{dh}"])
                        p.emit("vector", lambda e, dh=dh, ob=ob, po1=po1: e.tensor_copy(
                            out=ob[:, dh * 512:(dh + 1) * 512], in_=po1[:]),
                            reads=[f"g_po{dh}"], writes=[f"g_ot{i%2}"])
                    com.ln_tm(ob[:], f"g_ot{i%2}", lnout[:], "g_lnout")
                    b = xt[i % 2]
                    p.dma(b[:], xin[ti * 128:(ti + 1) * 128, :], f"gx{i%2}",
                          writes=[f"g_xt{i%2}"])
                    p.emit("vector", lambda e, b=b: e.tensor_tensor(
                        out=b[:], in0=b[:], in1=lnout[:], op=ALU.add),
                        reads=[f"g_xt{i%2}", "g_lnout"], writes=[f"g_xt{i%2}"])
                    p.dma(xout[ti * 128:(ti + 1) * 128, :], b[:], "gxo",
                          reads=[f"g_xt{i%2}"])



def host_consts():
    ident = np.eye(128, dtype=NBF16)
    ones_row = np.ones((1, 128), dtype=NBF16)
    smalls = np.zeros((128, 8), np.float32)
    smalls[:, 0] = EPS
    smalls[:, 1] = 1e-6
    smalls[:, 2] = 1.0
    masks = np.zeros((128, 3 * 128), NBF16)
    masks[:, 0:128] = np.tril(np.ones((128, 128), np.float32), -1).astype(NBF16)
    masks[:, 128:256] = np.triu(np.ones((128, 128), np.float32), 1).astype(NBF16)
    masks[:, 256:384] = np.triu(np.ones((128, 128), np.float32), 0).astype(NBF16)
    return {"c_ident": ident, "c_ones_row": ones_row, "c_smalls": smalls,
            "c_masks": masks, "c_idf32": np.eye(128, dtype=np.float32),
            "c_ones_rowf": np.ones((1, 128), np.float32),
            "c_triuif": np.triu(np.ones((128, 128), np.float32), 0)}


def const_dram(nc):
    cd = {}
    for k, v in host_consts().items():
        dt = F32 if v.dtype == np.float32 else BF16
        cd[k] = nc.dram_tensor("i_" + k, list(v.shape), dt, kind="ExternalInput")
    return cd


def build_l0ffn_only(T):
    nc = bass.Bass("TRN2", target_bir_lowering=False)
    H3 = 3072
    xin = nc.dram_tensor("xin", [T, D], F32, kind="ExternalInput")
    xout = nc.dram_tensor("xout", [T, D], F32, kind="ExternalOutput")
    wg = nc.dram_tensor("l0f_wg", [D, H3], I8, kind="ExternalInput")
    wu = nc.dram_tensor("l0f_wu", [D, H3], I8, kind="ExternalInput")
    wd = nc.dram_tensor("l0f_wd", [H3, D], I8, kind="ExternalInput")
    cd = const_dram(nc)
    p = Prog(nc)
    com = Common(p, cd)
    emit_l0_ffn(p, com, {"l0f_wg": wg, "l0f_wu": wu, "l0f_wd": wd}, xin, xout, T)
    p.run()
    return nc, p


def stage_l1ffn(inp):
    o = {}
    o["l1g_w1t"] = np.ascontiguousarray(inp["L1_ffn_gate_w1"].T)
    o["l1g_w2t"] = np.ascontiguousarray(inp["L1_ffn_gate_w2"].T)
    o["l1g_b1fm"] = np.ascontiguousarray(inp["L1_ffn_gate_b1"].reshape(8, 128).T)
    o["l1g_b2col"] = np.ascontiguousarray(inp["L1_ffn_gate_b2"].reshape(8, 1))
    o["l1f_w1t"] = np.ascontiguousarray(inp["L1_ffn_W1"].T)
    o["l1f_tw1t"] = np.ascontiguousarray(inp["L1_ffn_TW1"].transpose(0, 2, 1))
    o["l1f_w2t"] = np.ascontiguousarray(inp["L1_ffn_W2"].T)
    o["l1f_tw2t"] = np.ascontiguousarray(inp["L1_ffn_TW2"].transpose(0, 2, 1))
    o["l1f_b1fm"] = np.ascontiguousarray(inp["L1_ffn_b1"].reshape(32, 128).T)
    o["l1f_tb1fm"] = np.ascontiguousarray(
        inp["L1_ffn_Tb1"].reshape(8, 32, 128).transpose(2, 0, 1).reshape(128, 8 * 32))
    o["l1f_b2row"] = np.ascontiguousarray(inp["L1_ffn_b2"].reshape(1, D))
    o["l1f_tb2"] = np.ascontiguousarray(inp["L1_ffn_Tb2"])
    return o


L1FFN_SHAPES = {
    "l1g_w1t": ([D, D], F32), "l1g_w2t": ([D, 8], F32),
    "l1g_b1fm": ([128, 8], F32), "l1g_b2col": ([8, 1], F32),
    "l1f_w1t": ([D, 4096], F32), "l1f_tw1t": ([8, D, 4096], F32),
    "l1f_w2t": ([4096, D], F32), "l1f_tw2t": ([8, 4096, D], F32),
    "l1f_b1fm": ([128, 32], F32), "l1f_tb1fm": ([128, 8 * 32], F32),
    "l1f_b2row": ([1, D], F32), "l1f_tb2": ([8, D], F32),
}


def build_l1ffn_only(T):
    nc = bass.Bass("TRN2", target_bir_lowering=False)
    xin = nc.dram_tensor("xin", [T, D], F32, kind="ExternalInput")
    xout = nc.dram_tensor("xout", [T, D], F32, kind="ExternalOutput")
    W = {k: nc.dram_tensor(k, list(s), dt, kind="ExternalInput")
         for k, (s, dt) in L1FFN_SHAPES.items()}
    y3fm = nc.dram_tensor("y3fm", [D, T], BF16, kind="Internal")
    cd = const_dram(nc)
    p = Prog(nc)
    com = Common(p, cd)
    emit_l1_ffn(p, com, W, xin, xout, y3fm, T)
    p.run()
    return nc, p


def build_full(T=4096, dbg_taps=False):
    nc = bass.Bass("TRN2", target_bir_lowering=False)
    H3 = 3072
    xin = nc.dram_tensor("x0", [T, D], F32, kind="ExternalInput")
    xout = nc.dram_tensor("out", [T, D], F32, kind="ExternalOutput")
    shapes = {}
    shapes.update(L0ATTN_SHAPES)
    shapes.update({"l0f_wg": ([D, H3], I8), "l0f_wu": ([D, H3], I8),
                   "l0f_wd": ([H3, D], I8)})
    shapes.update(l1attn_shapes(T))
    shapes.update(L1FFN_SHAPES)
    W = {k: nc.dram_tensor(k, list(s), dt, kind="ExternalInput")
         for k, (s, dt) in shapes.items()}
    ikind = "ExternalOutput" if dbg_taps else "Internal"
    skind = "ExternalInput" if dbg_taps == "nan" else ("ExternalOutput" if dbg_taps else "Internal")
    if dbg_taps == "nan":
        ikind = "ExternalInput"
    x1d = nc.dram_tensor("x1d", [T, D], F32, kind=ikind)
    x2d = nc.dram_tensor("x2d", [T, D], F32, kind=ikind)
    x3d = nc.dram_tensor("x3d", [T, D], F32, kind=ikind)
    z1d = nc.dram_tensor("z1d", [T, D], F32, kind=skind)
    qd = nc.dram_tensor("qd", [D, T], BF16, kind=skind)
    od = nc.dram_tensor("od", [T, D], BF16, kind=skind)
    y3fm = nc.dram_tensor("y3fm", [D, T], BF16, kind=skind)
    cd = const_dram(nc)
    p = Prog(nc)
    com = Common(p, cd)
    emit_l0_attn(p, com, W, xin, x1d, z1d, T)
    emit_l0_ffn(p, com, W, x1d, x2d, T)
    emit_l1_attn(p, com, W, x2d, x3d, qd, od, T)
    emit_l1_ffn(p, com, W, x3d, xout, y3fm, T)
    p.run()
    return nc, p


def stage_all(inputs, T=4096):
    Wn = {}
    Wn.update(stage_l0attn(inputs))
    Wn["l0f_wg"] = np.ascontiguousarray(inputs["L0_ffn_w_gate"].T).astype(np.int8)
    Wn["l0f_wu"] = np.ascontiguousarray(inputs["L0_ffn_w_up"].T).astype(np.int8)
    Wn["l0f_wd"] = np.ascontiguousarray(inputs["L0_ffn_w_down"].T).astype(np.int8)
    Wn.update(stage_l1attn(inputs, T))
    Wn.update(stage_l1ffn(inputs))
    for k, v in host_consts().items():
        Wn["i_" + k] = v
    return Wn


def kernel(**inputs):
    T = 4096
    nc, p = build_full(T)
    Wn = stage_all(inputs, T)
    x0 = np.ascontiguousarray(inputs["L0_attn_x"].astype(np.float32))
    in_maps = []
    for b in range(8):
        m = dict(Wn)
        m["x0"] = np.ascontiguousarray(x0[b])
        in_maps.append(m)
    res = run_bass_kernel_spmd(nc, in_maps, core_ids=list(range(8)))
    return np.stack([res.results[b]["out"] for b in range(8)], axis=0)


def stage_l0attn(inp):
    o = {}
    o["l0a_wq"] = np.ascontiguousarray(inp["L0_attn_Wq"])
    o["l0a_wk"] = np.ascontiguousarray(inp["L0_attn_Wk"])
    o["l0a_wv"] = np.ascontiguousarray(inp["L0_attn_Wv"])
    o["l0a_wo"] = np.ascontiguousarray(inp["L0_attn_Wo"])
    o["l0a_wab"] = np.ascontiguousarray(
        np.concatenate([inp["L0_attn_Wb"], inp["L0_attn_Wa"]], axis=1))
    o["l0a_ln1"] = np.ascontiguousarray(np.concatenate(
        [inp["L0_attn_ln1_w"].reshape(8, 128).T, inp["L0_attn_ln1_b"].reshape(8, 128).T],
        axis=1))
    o["l0a_ln2"] = np.ascontiguousarray(np.concatenate(
        [inp["L0_attn_ln2_w"].reshape(8, 128).T, inp["L0_attn_ln2_b"].reshape(8, 128).T],
        axis=1))
    o["l0a_w1"] = np.ascontiguousarray(inp["L0_attn_W1"])
    o["l0a_b1fm"] = np.ascontiguousarray(inp["L0_attn_b1"].reshape(32, 128).T)
    o["l0a_w2"] = np.ascontiguousarray(inp["L0_attn_W2"])
    o["l0a_b2row"] = np.ascontiguousarray(inp["L0_attn_b2"].reshape(1, D))
    return o


L0ATTN_SHAPES = {
    "l0a_wq": ([D, D], F32), "l0a_wk": ([D, D], F32), "l0a_wv": ([D, D], F32),
    "l0a_wo": ([D, D], F32), "l0a_wab": ([D, 16], F32),
    "l0a_ln1": ([128, 16], F32), "l0a_ln2": ([128, 16], F32),
    "l0a_w1": ([D, 4096], F32), "l0a_b1fm": ([128, 32], F32),
    "l0a_w2": ([4096, D], F32), "l0a_b2row": ([1, D], F32),
}


def build_l0attn_only(T):
    nc = bass.Bass("TRN2", target_bir_lowering=False)
    xin = nc.dram_tensor("xin", [T, D], F32, kind="ExternalInput")
    xout = nc.dram_tensor("xout", [T, D], F32, kind="ExternalOutput")
    W = {k: nc.dram_tensor(k, list(s), dt, kind="ExternalInput")
         for k, (s, dt) in L0ATTN_SHAPES.items()}
    z1d = nc.dram_tensor("z1d", [T, D], F32, kind="Internal")
    cd = const_dram(nc)
    p = Prog(nc)
    com = Common(p, cd)
    emit_l0_attn(p, com, W, xin, xout, z1d, T)
    p.run()
    return nc, p


LAM = 0.01


def stage_l1attn(inp, T=4096):
    o = {}
    o["l1a_wq"] = np.ascontiguousarray(inp["L1_attn_W_q"])
    o["l1a_wk"] = np.ascontiguousarray(inp["L1_attn_W_k"])
    o["l1a_wv"] = np.ascontiguousarray(inp["L1_attn_W_v"])
    o["l1a_wo"] = np.ascontiguousarray(inp["L1_attn_W_o"])
    o["l1a_wdkv"] = np.ascontiguousarray(inp["L1_attn_W_down_kv"])
    o["l1a_wupk"] = np.ascontiguousarray(inp["L1_attn_W_up_k"])
    o["l1a_wupv"] = np.ascontiguousarray(inp["L1_attn_W_up_v"])
    t = np.arange(T, dtype=np.float64)
    o["l1a_ep"] = np.broadcast_to(np.exp(LAM * t).astype(NBF16), (128, T)).copy()
    o["l1a_em"] = np.broadcast_to((np.exp(-LAM * t) / 32.0).astype(NBF16), (128, T)).copy()
    dcr = np.zeros((128, 4 * 1024), NBF16)
    for j, d0 in enumerate((0, 128, 256, 384)):
        kt = np.arange(128)[:, None] + d0
        qt = np.arange(512)[None, :]
        tile = (np.exp(-LAM * np.abs(kt - qt)) / 32.0).astype(NBF16)
        dcr[:, j * 1024:j * 1024 + 512] = tile
        dcr[:, j * 1024 + 512:(j + 1) * 1024] = tile
    o["l1a_dcross"] = dcr
    return o


L1ATTN_SHAPES = {
    "l1a_wq": ([D, D], F32), "l1a_wk": ([D, D], F32), "l1a_wv": ([D, D], F32),
    "l1a_wo": ([D, D], F32), "l1a_wdkv": ([D, 256], F32),
    "l1a_wupk": ([256, D], F32), "l1a_wupv": ([256, D], F32),
}


def l1attn_shapes(T):
    s = dict(L1ATTN_SHAPES)
    s["l1a_ep"] = ([128, T], BF16)
    s["l1a_em"] = ([128, T], BF16)
    s["l1a_dcross"] = ([128, 4 * 1024], BF16)
    return s


def build_l1attn_only(T):
    nc = bass.Bass("TRN2", target_bir_lowering=False)
    xin = nc.dram_tensor("xin", [T, D], F32, kind="ExternalInput")
    xout = nc.dram_tensor("xout", [T, D], F32, kind="ExternalOutput")
    W = {k: nc.dram_tensor(k, list(s), dt, kind="ExternalInput")
         for k, (s, dt) in l1attn_shapes(T).items()}
    qd = nc.dram_tensor("qd", [D, T], BF16, kind="Internal")
    od = nc.dram_tensor("od", [T, D], BF16, kind="Internal")
    cd = const_dram(nc)
    p = Prog(nc)
    com = Common(p, cd)
    emit_l1_attn(p, com, W, xin, xout, qd, od, T)
    p.run()
    return nc, p
